# version 04
# baseline (speedup 1.0000x reference)
import math
import contextlib
import numpy as np

import concourse.bass as bass
import concourse.mybir as mybir
from concourse.bass_utils import run_bass_kernel_spmd
from concourse.alu_op_type import AluOpType as Op

F32 = mybir.dt.float32
BF16 = mybir.dt.bfloat16
AF = mybir.ActivationFunctionType
AX = mybir.AxisListType

DM = 2048
H = 16
KC = 16
FH = 5632
MC = 44
T = 512
HLO = 6
TH = T + HLO
NTILE = 2
TOK = NTILE * T
LN1E6 = math.log(1.0e6)
TWO_PI = 2.0 * math.pi
NW = float(FH * DM)


class Sched:
    ENGS = ("T", "V", "A", "G", "S")

    def __init__(self, sems):
        self.sems = sems
        self.cnt = {k: 0 for k in sems}
        self.seen = {e: {k: 0 for k in sems} for e in self.ENGS}
        self.plan = {e: [] for e in self.ENGS}
        self.own = {"T": "t", "V": "v", "A": "a", "G": "g", "S": "s"}

    def emit(self, eng, fn, deps=(), dma=None, sync_self=False):
        sem = dma if dma else self.own[eng]
        step = 16 if dma else 1
        self.plan[eng].append(([d for d in deps if d is not None], fn, sem,
                               step, sync_self))
        self.cnt[sem] += step
        return (sem, self.cnt[sem])

    def tok(self, sem):
        return (sem, self.cnt[sem])

    def flush(self, block):
        decs = {"T": block.tensor, "V": block.vector, "A": block.scalar,
                "G": block.gpsimd, "S": block.sync}
        for eng in self.ENGS:
            ops = self.plan[eng]
            if not ops:
                continue
            self.plan[eng] = []
            seen = self.seen[eng]
            sems = self.sems
            own = self.own[eng]

            def body(e, ops=ops, seen=seen, own=own, sems=sems):
                for deps, fn, sem, step, sself in ops:
                    for ds, dv in deps:
                        if (ds != own or sself) and seen[ds] < dv:
                            e.wait_ge(sems[ds], dv)
                            seen[ds] = dv
                    fn(e).then_inc(sems[sem], step)
            decs[eng](body)


def build_nc(dbg=False):
    nc = bass.Bass("TRN2", target_bir_lowering=False)
    dbg_specs = []

    def mkdump(name, shape, dt=F32):
        ap = nc.dram_tensor("dbg_" + name, shape, dt,
                            kind="ExternalOutput").ap()
        dbg_specs.append(name)
        return ap

    d_xt = nc.dram_tensor("xt", [DM, HLO + TOK], F32, kind="ExternalInput").ap()
    d_wq = nc.dram_tensor("wq", [DM, DM], F32, kind="ExternalInput").ap()
    d_wk = nc.dram_tensor("wk", [DM, DM], F32, kind="ExternalInput").ap()
    d_wv = nc.dram_tensor("wv", [DM, DM], F32, kind="ExternalInput").ap()
    d_wz = nc.dram_tensor("wz", [DM, DM], F32, kind="ExternalInput").ap()
    d_wo = nc.dram_tensor("wo", [DM, DM], F32, kind="ExternalInput").ap()
    d_wab = nc.dram_tensor("wab", [128, 2 * KC * H], F32, kind="ExternalInput").ap()
    d_cw = nc.dram_tensor("cw", [128, H * 12], F32, kind="ExternalInput").ap()
    d_nw = nc.dram_tensor("nw", [128, KC], F32, kind="ExternalInput").ap()
    d_pos0 = nc.dram_tensor("pos0", [128, 1], F32, kind="ExternalInput").ap()
    d_w1 = nc.dram_tensor("w1t", [FH, DM], F32, kind="ExternalInput").ap()
    d_w2 = nc.dram_tensor("w2t", [FH, DM], F32, kind="ExternalInput").ap()
    d_w3 = nc.dram_tensor("w3t", [DM, FH], F32, kind="ExternalInput").ap()
    d_out = nc.dram_tensor("out", [DM, TOK], F32, kind="ExternalOutput").ap()

    es = contextlib.ExitStack()

    def sb(name, shape, dt=F32):
        return es.enter_context(nc.sbuf_tensor(name, shape, dt))

    xt = sb("xts", [128, KC * TH])
    slot = sb("slot", [128, KC * TH])
    slot_bf = slot[:].bitcast(BF16)

    POOLW = 31770
    pool = sb("pool", [128, POOLW])
    pool_bf = pool[:].bitcast(BF16)

    ones_c = sb("ones_c", [128, 1])
    ones_r = sb("ones_r", [1, 128])
    half_c = sb("half_c", [128, 1])
    quar_c = sb("quar_c", [128, 1])
    iot_p = sb("iot_p", [128, 1])
    inv_c = sb("inv_c", [128, 1])
    pos0 = sb("pos0s", [128, 1])
    nwt = sb("nwt", [128, KC])
    cwt = sb("cwt", [128, H * 12])
    wab = sb("wabs", [128, 2 * KC * H])
    wab_h = sb("wab_h", [128, 2 * KC * H], BF16)
    wab_l = sb("wab_l", [128, 2 * KC * H], BF16)
    gam_c = sb("gam_c", [128, 3])
    negc = sb("negc", [128, 3])
    grow = sb("grow", [1, 8])
    ohB = sb("ohB", [128, 512])
    eye16 = sb("eye16", [16, 16])
    i16 = sb("i16", [16, 16])
    epsc = sb("epsc", [128, 3])
    grow2 = sb("grow2", [128, 3])
    fin = sb("fin", [1, 1])

    O_OUT = 0
    O_QKV = 8192
    O_WST = O_QKV + 3640
    O_WBF = O_WST + 2048
    O_GAM = O_WBF + 4096
    GACC = O_GAM + 2048
    O_G64 = GACC + 288
    O_ROW = O_G64 + 2048
    O_CROW = O_ROW + 2560
    O_SQ = O_CROW + 520
    O_CS = O_SQ + 520
    O_AB = O_CS + 1040
    O_LR = O_AB + 1040
    ATTN_END = O_LR + 2080
    assert ATTN_END <= POOLW, ATTN_END
    O_ZS = O_ROW + 1536
    O_GT = O_ROW + 2048

    F_HID = 0
    F_ST = 22528
    F_Q = F_ST + 4096
    F_SC = F_Q + 4096
    FFN_END = F_SC + 1024
    assert FFN_END <= POOLW, FFN_END
    W3_ST = F_ST
    W3_Q = F_ST + 2816

    def pf(off, n):
        return pool[:, off:off + n]

    def pbf(off_f32, n_bf16):
        return pool_bf[:, 2 * off_f32: 2 * off_f32 + n_bf16]

    sems = {}
    for name in ("t", "v", "a", "g", "s", "sd", "gd"):
        sems[name] = es.enter_context(nc.semaphore(name))
    S = Sched(sems)
    S.extra = None

    _orig_emit = S.emit

    def _emit2(eng, fn, deps=(), dma=None, sync_self=False):
        dl = list(deps)
        if S.extra is not None:
            dl.append(S.extra)
        return _orig_emit(eng, fn, deps=dl, dma=dma, sync_self=sync_self)
    if dbg:
        S.emit = _emit2

    def dump(name, src, dep, shape, dt=F32):
        if not dbg:
            return
        d = mkdump(name, shape, dt)
        tok = _orig_emit("G", lambda e, d=d, src=src: e.dma_start(
            out=d, in_=src), deps=[x for x in [dep, S.extra] if x], dma="gd")
        S.extra = tok

    PB = [es.enter_context(nc.psum_tensor("pb%d" % i, [128, 512], F32))
          for i in range(8)]

    def MM(e, out, lhsT, rhs, start, stop):
        return e.matmul(out, lhsT, rhs, start=start, stop=stop,
                        skip_group_check=True)

    blk = es.enter_context(nc.Block())

    def h_hi(ch, c0=0, n=TH):
        b = 2 * ch * TH
        return slot_bf[:, b + c0: b + c0 + n]

    def h_lo(ch, c0=0, n=TH):
        b = 2 * ch * TH + TH
        return slot_bf[:, b + c0: b + c0 + n]

    def ao_ch(ch):
        return slot[:, ch * TH: ch * TH + T]

    def xt_ch(ch, c0=0, n=TH):
        return xt[:, ch * TH + c0: ch * TH + c0 + n]

    S.emit("G", lambda e: e.memset(ones_c[:], 1.0))
    S.emit("G", lambda e: e.memset(ones_r[:], 1.0))
    S.emit("G", lambda e: e.memset(half_c[:], 0.5))
    S.emit("G", lambda e: e.memset(quar_c[:], 0.25))
    tok_g0 = S.emit("G", lambda e: e.iota(
        iot_p[:], pattern=[[0, 1]], base=0, channel_multiplier=1,
        allow_small_or_imprecise_dtypes=True))
    t_ohb = S.emit("G", lambda e: e.memset(ohB[:], 0.0))
    for s4 in range(4):
        t_ohb = S.emit("G", lambda e, s4=s4: e.memset(
            ohB[:, 128 * s4 + 32 * s4: 128 * s4 + 32 * s4 + 1], 1.0),
            deps=[t_ohb], sync_self=True)
    tk16 = S.emit("G", lambda e: e.iota(
        i16[:], pattern=[[1, 16]], base=0, channel_multiplier=0,
        allow_small_or_imprecise_dtypes=True))
    S.emit("V", lambda e: e.tensor_scalar(
        eye16[:], i16[:], iot_p[0:16, 0:1], None, op0=Op.is_equal),
        deps=[tk16, tok_g0])
    S.emit("G", lambda e: e.memset(epsc[:, 0:1], 1.0e-5))
    S.emit("G", lambda e: e.memset(epsc[:, 1:2], 1.0e-6))
    S.emit("G", lambda e: e.memset(epsc[:, 2:3], 0.5e-8))
    S.emit("S", lambda e: e.dma_start(out=pos0[:], in_=d_pos0), dma="sd")
    S.emit("S", lambda e: e.dma_start(out=nwt[:], in_=d_nw), dma="sd")
    S.emit("S", lambda e: e.dma_start(out=cwt[:], in_=d_cw), dma="sd")
    tok_small = S.emit("S", lambda e: e.dma_start(out=wab[:], in_=d_wab),
                       dma="sd")
    th_ = S.emit("A", lambda e: e.activation(wab_h[:], wab[:], AF.Copy),
                 deps=[tok_small])
    tok_wab = S.emit("V", lambda e: e.tensor_tensor(
        wab_l[:], wab[:], wab_h[:], op=Op.subtract), deps=[th_])
    tge = S.emit("V", lambda e: e.tensor_scalar(
        grow[0:1, 0:1] if False else inv_c[:], iot_p[:], 64.0, None,
        op0=Op.is_ge), deps=[tok_g0])
    tiv = S.emit("V", lambda e: e.scalar_tensor_tensor(
        grow2[:, 0:1], inv_c[:], -64.0, iot_p[:], op0=Op.mult, op1=Op.add),
        deps=[tge], sync_self=True)
    tok_inv = S.emit("V", lambda e: e.memset(inv_c[:], 1.0), deps=[tge],
                     sync_self=True)
    for j in range(5, -1, -1):
        p2 = float(1 << j)
        r2j = float(1.0e6 ** (-(1 << j) / 64.0))
        tb = S.emit("V", lambda e, p2=p2: e.tensor_scalar(
            grow2[:, 1:2], grow2[:, 0:1], p2, None, op0=Op.is_ge),
            deps=[tiv, tok_inv], sync_self=True)
        tiv = S.emit("V", lambda e, p2=p2: e.scalar_tensor_tensor(
            grow2[:, 0:1], grow2[:, 1:2], -p2, grow2[:, 0:1], op0=Op.mult,
            op1=Op.add), deps=[tb], sync_self=True)
        tf = S.emit("V", lambda e, r2j=r2j: e.tensor_scalar(
            grow2[:, 2:3], grow2[:, 1:2], r2j - 1.0, 1.0, op0=Op.mult,
            op1=Op.add), deps=[tiv], sync_self=True)
        tok_inv = S.emit("V", lambda e: e.tensor_tensor(
            inv_c[:], inv_c[:], grow2[:, 2:3], op=Op.mult), deps=[tf],
            sync_self=True)

    gam_jobs = []
    idx = 0
    for dram, rows in ((d_w1, FH), (d_w2, FH), (d_w3, DM)):
        cols = DM if rows == FH else FH
        for rc in range(rows // 128):
            for cs in range(0, cols, 1024):
                w = min(1024, cols - cs)
                gam_jobs.append((dram, rc * 128, cs, w, idx))
                idx += 1
    N_GAM = len(gam_jobs)
    GLIM = {"w1": 88, "w2": 176, "w3": N_GAM}
    assert N_GAM <= 288 - 16, N_GAM
    gst = {"i": 0, "last": [None, None]}

    def gamma_step(nmax):
        for _ in range(nmax):
            i = gst["i"]
            if i >= N_GAM:
                return
            gst["i"] = i + 1
            dram, r0, c0, w, col = gam_jobs[i]
            buf = pf(O_GAM + (i % 2) * 1024, w)
            td = S.emit("G", lambda e, dram=dram, r0=r0, c0=c0, w=w, buf=buf:
                        e.dma_start(out=buf, in_=dram[r0:r0 + 128, c0:c0 + w]),
                        deps=[gst["last"][i % 2]], dma="gd")
            gst["last"][i % 2] = S.emit(
                "A", lambda e, buf=buf, col=col: e.activation(
                    buf, buf, AF.Abs, accum_out=pf(GACC + col, 1)), deps=[td])

    wstate = {"split": None, "mmfree": [None, None]}

    def load_w_slab(dram, m, par, extra=()):
        stg = pf(O_WST, 2048)
        whi = pbf(O_WBF + par * 2048, 2048)
        wlo = pbf(O_WBF + par * 2048 + 1024, 2048)
        td = S.emit("S", lambda e, dram=dram, m=m, stg=stg: e.dma_start(
            out=stg, in_=dram[m * 128:(m + 1) * 128, :]),
            deps=[wstate["split"]] + list(extra), dma="sd")
        tH = S.emit("A", lambda e, whi=whi, stg=stg: e.activation(
            whi, stg, AF.Copy), deps=[td, wstate["mmfree"][par]])
        tL = S.emit("V", lambda e, wlo=wlo, stg=stg, whi=whi: e.tensor_tensor(
            wlo, stg, whi, op=Op.subtract),
            deps=[tH, wstate["mmfree"][par]])
        wstate["split"] = tL
        return whi, wlo, [tH, tL]

    def mm3(out, whi, wlo, ch, c0, n, start, stop, deps=()):
        S.emit("T", lambda e: MM(e, out, whi, h_hi(ch, c0, n), start, False),
               deps=deps)
        S.emit("T", lambda e: MM(e, out, whi, h_lo(ch, c0, n), False, False))
        return S.emit("T", lambda e: MM(e, out, wlo, h_hi(ch, c0, n), False,
                                        stop))

    def layernorm_to(grid_full, dep_first, out_hi, out_lo):
        c0 = 0 if grid_full else HLO
        n = TH if grid_full else T
        sqb = [pf(O_SQ, 520), pf(O_AB + 520, 520)]
        lastq = [None, None]
        for ch in range(KC):
            src = xt_ch(ch, c0, n)
            sq = sqb[ch % 2]
            S.emit("T", lambda e, src=src, ch=ch: MM(
                e, PB[4][0:1, 0:512], ones_c[:], src[:, 0:512],
                ch == 0, ch == KC - 1), deps=dep_first if ch == 0 else [])
            tq = S.emit("V", lambda e, src=src, n=n, sq=sq: e.tensor_tensor(
                sq[:, 0:n], src, src, op=Op.mult),
                deps=(dep_first if ch == 0 else []) + [lastq[ch % 2]])
            S.emit("T", lambda e, ch=ch, sq=sq: MM(
                e, PB[5][0:1, 0:512], ones_c[:], sq[:, 0:512],
                ch == 0, ch == KC - 1), deps=[tq])
            if grid_full:
                S.emit("T", lambda e, src=src, ch=ch: MM(
                    e, PB[6][0:1, 0:6], ones_c[:], src[:, 512:518],
                    ch == 0, ch == KC - 1))
                lastq[ch % 2] = S.emit("T", lambda e, ch=ch, sq=sq: MM(
                    e, PB[7][0:1, 0:6], ones_c[:], sq[:, 512:518],
                    ch == 0, ch == KC - 1), deps=[tq])
            else:
                lastq[ch % 2] = S.tok("t")
        tmm = S.tok("t")
        mu = pf(O_LR, 520)[0:1, 0:n]
        msq = pf(O_LR + 520, 520)[0:1, 0:n]
        var = pf(O_LR + 1040, 520)[0:1, 0:n]
        ist = pf(O_LR + 1560, 520)[0:1, 0:n]
        S.emit("A", lambda e, mu=mu: e.activation(
            mu[:, 0:512], PB[4][0:1, 0:512], AF.Copy, scale=1.0 / DM),
            deps=[tmm])
        ta = S.emit("A", lambda e, msq=msq: e.activation(
            msq[:, 0:512], PB[5][0:1, 0:512], AF.Copy, scale=1.0 / DM),
            deps=[tmm])
        if grid_full:
            tev1 = S.emit("A", lambda e, mu=mu: e.activation(
                mu[:, 512:518], PB[6][0:1, 0:6], AF.Copy, scale=1.0 / DM),
                deps=[tmm])
            ta = S.emit("A", lambda e, msq=msq: e.activation(
                msq[:, 512:518], PB[7][0:1, 0:6], AF.Copy, scale=1.0 / DM),
                deps=[tmm])
        tv = S.emit("V", lambda e, var=var, mu=mu: e.tensor_tensor(
            var, mu, mu, op=Op.mult), deps=[ta])
        tv = S.emit("V", lambda e, var=var, msq=msq: e.tensor_tensor(
            var, msq, var, op=Op.subtract), deps=[tv])
        tv = S.emit("V", lambda e, var=var: e.tensor_scalar(
            var, var, 1.0, 1.0e-5, op0=Op.mult, op1=Op.add), deps=[tv])
        tis = S.emit("A", lambda e, ist=ist, var=var: e.activation(
            ist, var, AF.Ln), deps=[tv])
        tis = S.emit("A", lambda e, ist=ist: e.activation(
            ist, ist, AF.Exp, scale=-0.5), deps=[tis])
        tis = S.emit("V", lambda e, ist=ist, msq=msq: e.tensor_tensor(
            msq, ist, ist, op=Op.mult), deps=[tis])
        tis = S.emit("V", lambda e, msq=msq, var=var: e.tensor_tensor(
            msq, msq, var, op=Op.mult), deps=[tis])
        tis = S.emit("V", lambda e, msq=msq: e.tensor_scalar(
            msq, msq, -0.5, 1.5, op0=Op.mult, op1=Op.add), deps=[tis])
        tis = S.emit("V", lambda e, ist=ist, msq=msq: e.tensor_tensor(
            ist, ist, msq, op=Op.mult), deps=[tis])
        S.emit("T", lambda e, mu=mu: MM(
            e, PB[2][:, 0:512], ones_r[:], mu[:, 0:512], True, True),
            deps=[tis])
        S.emit("T", lambda e, ist=ist: MM(
            e, PB[3][:, 0:512], ones_r[:], ist[:, 0:512], True, True),
            deps=[tis])
        if grid_full:
            S.emit("T", lambda e, mu=mu: MM(
                e, PB[6][:, 0:6], ones_r[:], mu[:, 512:518], True, True),
                deps=[tis])
            S.emit("T", lambda e, ist=ist: MM(
                e, PB[7][:, 0:6], ones_r[:], ist[:, 512:518], True, True),
                deps=[tis])
        tbc = S.tok("t")
        scr = pf(O_SQ, n)
        for ch in range(KC):
            src = xt_ch(ch, c0, n)
            tm = S.emit("V", lambda e, src=src: e.tensor_tensor(
                scr[:, 0:512], src[:, 0:512], PB[2][:, 0:512],
                op=Op.subtract), deps=[tbc])
            tm = S.emit("V", lambda e: e.tensor_tensor(
                scr[:, 0:512], scr[:, 0:512], PB[3][:, 0:512], op=Op.mult),
                deps=[tm])
            if grid_full:
                tm2 = S.emit("V", lambda e, src=src: e.tensor_tensor(
                    scr[:, 512:518], src[:, 512:518], PB[6][:, 0:6],
                    op=Op.subtract), deps=[tbc])
                tm = S.emit("V", lambda e: e.tensor_tensor(
                    scr[:, 512:518], scr[:, 512:518], PB[7][:, 0:6],
                    op=Op.mult), deps=[tm2, tm], sync_self=True)
            thc = S.emit("A", lambda e, ch=ch: e.activation(
                out_hi(ch), scr, AF.Copy), deps=[tm])
            S.emit("V", lambda e, ch=ch: e.tensor_tensor(
                out_lo(ch), scr, out_hi(ch), op=Op.subtract), deps=[thc])
        return S.tok("v"), S.tok("a")

    def resid_ln(halfed, dep_first):
        oc = half_c if halfed else ones_c
        oq = quar_c if halfed else ones_c
        sqb = [pf(O_SQ, 520), pf(O_AB + 520, 520)]
        lastq = [None, None]
        for ch in range(KC):
            src = ao_ch(ch)
            sq = sqb[ch % 2]
            S.emit("T", lambda e, src=src, ch=ch, oc=oc: MM(
                e, PB[4][0:1, 0:512], oc[:], src, ch == 0, ch == KC - 1),
                deps=dep_first if ch == 0 else [])
            tq = S.emit("V", lambda e, src=src, sq=sq: e.tensor_tensor(
                sq[:, 0:T], src, src, op=Op.mult),
                deps=(dep_first if ch == 0 else []) + [lastq[ch % 2]])
            lastq[ch % 2] = S.emit("T", lambda e, ch=ch, oq=oq, sq=sq: MM(
                e, PB[5][0:1, 0:512], oq[:], sq[:, 0:T], ch == 0,
                ch == KC - 1), deps=[tq])
        tmm = S.tok("t")
        mu = pf(O_LR, 520)[0:1, 0:T]
        msq = pf(O_LR + 520, 520)[0:1, 0:T]
        var = pf(O_LR + 1040, 520)[0:1, 0:T]
        ist = pf(O_LR + 1560, 520)[0:1, 0:T]
        S.emit("A", lambda e, mu=mu: e.activation(
            mu, PB[4][0:1, 0:512], AF.Copy, scale=1.0 / DM), deps=[tmm])
        ta = S.emit("A", lambda e, msq=msq: e.activation(
            msq, PB[5][0:1, 0:512], AF.Copy, scale=1.0 / DM), deps=[tmm])
        tv = S.emit("V", lambda e, var=var, mu=mu: e.tensor_tensor(
            var, mu, mu, op=Op.mult), deps=[ta])
        tv = S.emit("V", lambda e, var=var, msq=msq: e.tensor_tensor(
            var, msq, var, op=Op.subtract), deps=[tv])
        tv = S.emit("V", lambda e, var=var: e.tensor_scalar(
            var, var, 1.0, 1.0e-5, op0=Op.mult, op1=Op.add), deps=[tv])
        tis = S.emit("A", lambda e, ist=ist, var=var: e.activation(
            ist, var, AF.Ln), deps=[tv])
        tis = S.emit("A", lambda e, ist=ist: e.activation(
            ist, ist, AF.Exp, scale=-0.5), deps=[tis])
        tis = S.emit("V", lambda e, ist=ist, msq=msq: e.tensor_tensor(
            msq, ist, ist, op=Op.mult), deps=[tis])
        tis = S.emit("V", lambda e, msq=msq, var=var: e.tensor_tensor(
            msq, msq, var, op=Op.mult), deps=[tis])
        tis = S.emit("V", lambda e, msq=msq: e.tensor_scalar(
            msq, msq, -0.5, 1.5, op0=Op.mult, op1=Op.add), deps=[tis])
        tis = S.emit("V", lambda e, ist=ist, msq=msq: e.tensor_tensor(
            ist, ist, msq, op=Op.mult), deps=[tis])
        S.emit("T", lambda e, mu=mu: MM(
            e, PB[2][:, 0:512], ones_r[:], mu, True, True), deps=[tis])
        S.emit("T", lambda e, ist=ist: MM(
            e, PB[3][:, 0:512], ones_r[:], ist, True, True), deps=[tis])
        tbc = S.tok("t")
        lastv = None
        for ch in range(KC):
            src = ao_ch(ch)
            xdst = xt_ch(ch, HLO, T)
            if halfed:
                tm = S.emit("V", lambda e, src=src: e.scalar_tensor_tensor(
                    pf(O_SQ, T), src, 0.5, PB[2][:, 0:512], op0=Op.mult,
                    op1=Op.subtract), deps=[tbc])
            else:
                tm = S.emit("V", lambda e, src=src: e.tensor_tensor(
                    pf(O_SQ, T), src, PB[2][:, 0:512], op=Op.subtract),
                    deps=[tbc])
            tm = S.emit("V", lambda e: e.tensor_tensor(
                pf(O_SQ, T), pf(O_SQ, T), PB[3][:, 0:512], op=Op.mult),
                deps=[tm])
            lastv = S.emit("V", lambda e, xdst=xdst: e.tensor_tensor(
                xdst, xdst, pf(O_SQ, T), op=Op.add), deps=[tm])
        return lastv

    barrier = []
    for tile in range(NTILE):
        for ch in range(KC):
            src = d_xt[ch * 128:(ch + 1) * 128, tile * T: tile * T + TH]
            S.emit("G", lambda e, ch=ch, src=src: e.dma_start(
                out=xt_ch(ch), in_=src),
                deps=barrier if ch == 0 else [], dma="gd")
        tok_xload = S.tok("gd")

        cosb = pf(O_CS, TH)
        sinb = pf(O_CS + 520, TH)
        u = pf(O_AB, TH)
        uscr = pf(O_AB + 520, TH)
        t0 = S.emit("G", lambda e: e.iota(
            u, pattern=[[1, TH]], base=0, channel_multiplier=0,
            allow_small_or_imprecise_dtypes=True), deps=barrier)
        tl_ = S.emit("V", lambda e, tile=tile: e.tensor_scalar(
            u, u, pos0[:, 0:1], float(tile * T), op0=Op.add, op1=Op.add),
            deps=[t0, tok_small] + barrier)
        tl_ = S.emit("V", lambda e: e.tensor_scalar(
            u, u, inv_c[:, 0:1], None, op0=Op.mult), deps=[tl_, tok_inv])
        tl_ = S.emit("V", lambda e: e.tensor_scalar(
            u, u, 1.0 / TWO_PI, 1024.0, op0=Op.mult, op1=Op.add), deps=[tl_])
        for j in range(10, -1, -1):
            p2 = float(1 << j)
            tl_ = S.emit("V", lambda e, p2=p2: e.tensor_scalar(
                uscr, u, p2, None, op0=Op.is_ge), deps=[tl_])
            tl_ = S.emit("V", lambda e, p2=p2: e.scalar_tensor_tensor(
                u, uscr, -p2, u, op0=Op.mult, op1=Op.add), deps=[tl_])
        tl_ = S.emit("V", lambda e: e.tensor_scalar(
            uscr, u, 0.5, None, op0=Op.is_ge), deps=[tl_])
        tl_ = S.emit("V", lambda e: e.scalar_tensor_tensor(
            u, uscr, -1.0, u, op0=Op.mult, op1=Op.add), deps=[tl_])
        if tile == 0:
            dump("invc", inv_c[:], tok_inv, [128, 1])
            dump("mrow", u, tl_, [128, TH])
        tsn = S.emit("A", lambda e: e.activation(sinb, u, AF.Sin,
                                                 scale=TWO_PI), deps=[tl_])
        tl_ = S.emit("V", lambda e: e.tensor_scalar(
            u, u, -1.0, 0.25, op0=Op.mult, op1=Op.add), deps=[tsn])
        tl_ = S.emit("V", lambda e: e.tensor_scalar(
            uscr, u, 0.5, None, op0=Op.is_ge), deps=[tl_])
        tl_ = S.emit("V", lambda e: e.scalar_tensor_tensor(
            u, uscr, -1.0, u, op0=Op.mult, op1=Op.add), deps=[tl_])
        tok_tab = S.emit("A", lambda e: e.activation(
            cosb, u, AF.Sin, scale=TWO_PI), deps=[tl_])
        if tile == 0:
            dump("cos", cosb, tok_tab, [128, TH])
            dump("sin", sinb, tok_tab, [128, TH])

        tok_h_v, tok_h_a = layernorm_to(
            True, [tok_xload] + barrier, lambda ch: h_hi(ch),
            lambda ch: h_lo(ch))
        if tile == 0:
            dump("xt0", xt_ch(0), tok_xload, [128, TH])
            dump("murow", pf(O_LR, 2080), tok_h_v, [128, 2080])
            dump("istd1", pf(O_LR + 1560, 518), tok_h_v, [128, 518])
            dump("hh0", h_hi(0), tok_h_v, [128, TH], BF16)
            dump("hl0", h_lo(0), tok_h_v, [128, TH], BF16)

        for wi in range(2):
            dst = PB[7][0:16, 0:512] if wi == 0 else PB[6][0:16, 0:512]
            for ch in range(KC):
                whi = wab_h[:, wi * KC * H + ch * H: wi * KC * H + (ch + 1) * H]
                wlo = wab_l[:, wi * KC * H + ch * H: wi * KC * H + (ch + 1) * H]
                st, sp = ch == 0, ch == KC - 1
                deps = [tok_wab, tok_h_v, tok_h_a] if ch == 0 else []
                S.emit("T", lambda e, dst=dst, whi=whi, ch=ch, st=st: MM(
                    e, dst, whi, h_hi(ch, HLO, 512), st, False), deps=deps)
                S.emit("T", lambda e, dst=dst, whi=whi, ch=ch: MM(
                    e, dst, whi, h_lo(ch, HLO, 512), False, False))
                S.emit("T", lambda e, dst=dst, wlo=wlo, ch=ch, sp=sp: MM(
                    e, dst, wlo, h_hi(ch, HLO, 512), False, sp))
                if wi == 0:
                    tdst = PB[5][0:16, 0:3]
                    S.emit("T", lambda e, tdst=tdst, whi=whi, ch=ch, st=st: MM(
                        e, tdst, whi, h_hi(ch, 3, 3), st, False))
                    S.emit("T", lambda e, tdst=tdst, whi=whi, ch=ch: MM(
                        e, tdst, whi, h_lo(ch, 3, 3), False, False))
                    S.emit("T", lambda e, tdst=tdst, wlo=wlo, ch=ch, sp=sp: MM(
                        e, tdst, wlo, h_hi(ch, 3, 3), False, sp))
        tabm = S.tok("t")
        abuf = pf(O_AB, 520)
        bbuf = pf(O_AB + 520, 520)
        S.emit("A", lambda e: e.activation(abuf[0:H, HLO:TH],
                                           PB[7][0:16, 0:512], AF.Copy),
               deps=[tabm, tok_tab])
        S.emit("A", lambda e: e.activation(abuf[0:H, 3:6], PB[5][0:16, 0:3],
                                           AF.Copy), deps=[tabm])
        tok_ab = S.emit("A", lambda e: e.activation(
            bbuf[0:H, HLO:TH], PB[6][0:16, 0:512], AF.Copy), deps=[tabm])

        g64 = pf(O_G64, 2048)
        tg = []
        for s in range(4):
            tg.append(S.emit("V", lambda e, s=s: e.tensor_tensor(
                g64[0:H, 512 * s:512 * s + 512], abuf[0:H, HLO - s:TH - s],
                bbuf[0:H, HLO:TH], op=Op.mult), deps=[tok_ab]))
        tok_g64 = S.emit("A", lambda e: e.activation(
            g64[0:H, :], g64[0:H, :], AF.Sigmoid), deps=tg)

        qb = pf(O_QKV, 520)
        kb = pf(O_QKV + 520, 520)
        vb = pf(O_QKV + 1040, 520)
        qc = pf(O_QKV + 1560, 520)
        kc_ = pf(O_QKV + 2080, 520)
        prodb = pf(O_QKV + 2600, 520)
        vpost = pf(O_QKV + 3120, 520)
        D4e = pf(O_ROW, 512)
        RQ4 = pf(O_ROW + 512, 512)
        RK4 = pf(O_ROW + 1024, 512)
        G4 = pf(O_ROW + 1536, 512)
        KT = pf(O_ROW + 2048, 8)
        crow2 = pf(O_CROW, 512)
        hs = {}
        wstate["split"] = S.tok("v")
        wstate["mmfree"] = [None, None]

        SPECS = [("q", None, 0, 0), ("k", None, 1, 1), ("v", None, 0, 0)]

        def front_mm(h, idx):
            nm, dram, par, bank = [("q", d_wq, 0, 0), ("k", d_wk, 1, 1),
                                   ("v", d_wv, 0, 0)][idx]
            if idx == 2:
                wstate["mmfree"][0] = hs[("mm", h, 0)]
            whi, wlo, wdeps = load_w_slab(
                dram, h, par, extra=[tok_h_a] if h == 0 and idx == 0 else [])
            psm = PB[bank]
            prev_evac = (hs.get(("ev", h, 0)) if idx == 2 else
                         hs.get(("ev", h - 1, 2)) if idx == 0 else
                         hs.get(("ev", h - 1, 1)))
            tl2 = None
            for ch in range(KC):
                whic = whi[:, ch * 128:(ch + 1) * 128]
                wloc = wlo[:, ch * 128:(ch + 1) * 128]
                tl2 = mm3(psm[:, 0:512], whic, wloc, ch, 0, 512,
                          ch == 0, ch == KC - 1,
                          deps=(wdeps + [tok_h_v, tok_h_a, prev_evac])
                          if ch == 0 else [])
            hs[("tmain", h, idx)] = tl2
            for ch in range(KC):
                whic = whi[:, ch * 128:(ch + 1) * 128]
                wloc = wlo[:, ch * 128:(ch + 1) * 128]
                tl2 = mm3(PB[5][:, idx * 8: idx * 8 + 6], whic, wloc,
                          ch, 512, 6, ch == 0, ch == KC - 1,
                          deps=[tok_ab] if h == 0 and idx == 0 and
                          ch == 0 else [])
            hs[("mm", h, idx)] = tl2
            if idx == 1:
                wstate["mmfree"][1] = tl2
            if idx == 2:
                wstate["mmfree"][0] = tl2

        def front_evac(h, idx):
            nm = ("q", "k", "v")[idx]
            buf = {"q": qb, "k": kb, "v": vb}[nm]
            psm = PB[(0, 1, 0)[idx]]
            freed = (hs.get(("useqk", h - 1)) if idx in (0, 1) else
                     hs.get(("usev", h - 1)))
            S.emit("A", lambda e, buf=buf, psm=psm: e.activation(
                buf[:, 0:512], psm[:, 0:512], AF.Silu),
                deps=[hs[("tmain", h, idx)], freed])
            hs[("ev", h, idx)] = S.emit(
                "A", lambda e, buf=buf, idx=idx: e.activation(
                    buf[:, 512:518], PB[5][:, idx * 8: idx * 8 + 6],
                    AF.Silu), deps=[hs[("mm", h, idx)], freed])
            if idx == 2:
                hs[("front", h)] = S.tok("a")

        def conv_rope(h, pre, post, rotbuf, cwcol, do_rope, dep0):
            tc = S.emit("V", lambda e, pre=pre, post=post, cwcol=cwcol:
                        e.tensor_scalar(post[:, 0:TH], pre[:, 0:TH],
                                        cwt[:, cwcol + 3:cwcol + 4], None,
                                        op0=Op.mult), deps=dep0)
            for j in range(3):
                sft = 3 - j
                tc = S.emit("V", lambda e, pre=pre, post=post, j=j, sft=sft,
                            cwcol=cwcol: e.scalar_tensor_tensor(
                    post[:, sft:TH], pre[:, 0:TH - sft],
                    cwt[:, cwcol + j:cwcol + j + 1], post[:, sft:TH],
                    op0=Op.mult, op1=Op.add), deps=[tc])
            if not do_rope:
                return tc
            tc = S.emit("V", lambda e, post=post, rotbuf=rotbuf:
                        e.tensor_scalar(rotbuf[0:64, 3:TH],
                                        post[64:128, 3:TH], -1.0, None,
                                        op0=Op.mult), deps=[tc])
            tc = S.emit("V", lambda e, post=post, rotbuf=rotbuf:
                        e.tensor_copy(rotbuf[64:128, 3:TH], post[0:64, 3:TH]),
                        deps=[tc])
            tc = S.emit("V", lambda e, post=post: e.tensor_tensor(
                post[:, 3:TH], post[:, 3:TH], cosb[:, 3:TH], op=Op.mult),
                deps=[tc, tok_tab])
            tc = S.emit("V", lambda e, rotbuf=rotbuf: e.tensor_tensor(
                rotbuf[:, 3:TH], rotbuf[:, 3:TH], sinb[:, 3:TH], op=Op.mult),
                deps=[tc])
            tc = S.emit("V", lambda e, post=post, rotbuf=rotbuf:
                        e.tensor_tensor(post[:, 3:TH], post[:, 3:TH],
                                        rotbuf[:, 3:TH], op=Op.add), deps=[tc])
            return tc

        def head_back(h):
            dep0 = [hs[("front", h)]]
            cwb = h * 12
            if tile == 0 and h == 0:
                dump("qpre0", qb, hs[("front", 0)], [128, 520])
                dump("kpre0", kb, hs[("front", 0)], [128, 520])
                dump("vpre0", vb, hs[("front", 0)], [128, 520])
            tq = conv_rope(h, qb, qc, prodb, cwb, True, dep0)
            tsq = S.emit("V", lambda e: e.tensor_tensor(
                pf(O_SQ, TH), qc[:, 0:TH], qc[:, 0:TH], op=Op.mult), deps=[tq])
            tl2q = S.emit("T", lambda e: MM(
                e, PB[6][0:1, 0:512], ones_c[:], pf(O_SQ + 6, 512), True,
                True), deps=[tsq, hs.get(("l2e", h - 1)),
                             hs.get(("gex", h - 1)),
                             tok_ab if h == 0 else None])
            tk = conv_rope(h, kb, kc_, qb, cwb + 4, True, [tq])
            tsk = S.emit("V", lambda e: e.tensor_tensor(
                pf(O_SQ, TH), kc_[:, 0:TH], kc_[:, 0:TH], op=Op.mult),
                deps=[tk, S.tok("t")])
            S.emit("T", lambda e: MM(
                e, PB[7][0:1, 0:512], ones_c[:], pf(O_SQ + 6, 512), True,
                True), deps=[tsk, hs.get(("l2e", h - 1)),
                             hs.get(("gex", h - 1)),
                             tok_ab if h == 0 else None])
            tl2k = S.emit("T", lambda e: MM(
                e, PB[5][0:1, 24:27], ones_c[:], pf(O_SQ + 3, 3), True,
                True), deps=[tsk, hs.get(("l2e", h - 1))])
            tv_ = conv_rope(h, vb, vpost, None, cwb + 8, False, [tk])
            hs[("useqk", h)] = tk
            hs[("usev", h)] = tv_
            rsqdep = [hs.get(("rsq", h - 1)), hs.get(("coef", h - 1))]
            te1 = S.emit("A", lambda e: e.activation(
                RQ4[0:1, 0:512], PB[6][0:1, 0:512], AF.Copy),
                deps=[tl2q] + rsqdep)
            te2 = S.emit("A", lambda e: e.activation(
                RK4[0:1, 0:512], PB[7][0:1, 0:512], AF.Copy),
                deps=[tl2k] + rsqdep)
            te4 = S.emit("A", lambda e: e.activation(
                KT[0:1, 0:3], PB[5][0:1, 24:27], AF.Copy), deps=[tl2k])
            hs[("l2e", h)] = S.tok("a")
            tq4 = []
            for s in range(1, 4):
                tq4.append(S.emit("V", lambda e, s=s: e.tensor_copy(
                    RQ4[32 * s:32 * s + 1, 0:512], RQ4[0:1, 0:512]),
                    deps=[te1] + rsqdep))
                tq4.append(S.emit("V", lambda e, s=s: e.tensor_copy(
                    RK4[32 * s:32 * s + 1, s:512], RK4[0:1, 0:512 - s]),
                    deps=[te2] + rsqdep))
                tq4.append(S.emit("V", lambda e, s=s: e.tensor_copy(
                    RK4[32 * s:32 * s + 1, 0:s], KT[0:1, 3 - s:3]),
                    deps=[te4] + rsqdep))
            tmx = S.emit("V", lambda e: e.tensor_scalar(
                pf(O_ROW + 512, 1024), pf(O_ROW + 512, 1024),
                1.0e-24, None, op0=Op.max), deps=tq4 + [te1, te2],
                sync_self=True)
            tr2 = tmx
            for blk_i, (ublk, fold) in enumerate(
                    ((RQ4, 1.0 / math.sqrt(128.0)), (RK4, 1.0))):
                ta1 = S.emit("A", lambda e, ublk=ublk: e.activation(
                    crow2[:, 0:512], ublk, AF.Ln),
                    deps=[tr2, hs.get(("lastst", h - 1)),
                          hs.get(("bc", h - 1, 3))])
                ta1 = S.emit("A", lambda e: e.activation(
                    D4e, crow2[:, 0:512], AF.Exp, scale=-0.5),
                    deps=[ta1, hs.get(("lastst", h - 1)),
                          hs.get(("d4f", h - 1))])
                tn = S.emit("V", lambda e: e.tensor_tensor(
                    crow2[:, 0:512], D4e, D4e, op=Op.mult), deps=[ta1])
                tn = S.emit("V", lambda e, ublk=ublk: e.tensor_tensor(
                    crow2[:, 0:512], crow2[:, 0:512], ublk, op=Op.mult),
                    deps=[tn])
                tn = S.emit("V", lambda e: e.tensor_scalar(
                    crow2[:, 0:512], crow2[:, 0:512], -0.5, 1.5,
                    op0=Op.mult, op1=Op.add), deps=[tn])
                tr2 = S.emit("V", lambda e, ublk=ublk, fold=fold:
                             e.scalar_tensor_tensor(
                                 ublk, D4e, fold, crow2[:, 0:512],
                                 op0=Op.mult, op1=Op.mult), deps=[tn])
            hs[("rsq", h)] = tr2
            tgk = []
            gexd = [hs.get(("l2e", h)), hs.get(("coef", h - 1))]
            for s in range(4):
                gps = PB[6 + (s % 2)]
                tgm = S.emit("T", lambda e, gps=gps, s=s, h=h: MM(
                    e, gps[0:1, 0:512], eye16[0:16, h:h + 1],
                    g64[0:16, 512 * s:512 * s + 512], True, True),
                    deps=[tok_g64, te1, te2] + gexd +
                    ([tgk[s - 2]] if s >= 2 else []))
                tgk.append(S.emit("A", lambda e, gps=gps, s=s: e.activation(
                    G4[32 * s:32 * s + 1, 0:512], gps[0:1, 0:512], AF.Copy),
                    deps=[tgm]))
            hs[("gex", h)] = S.tok("a")
            tdm = None
            d4dep = [hs.get(("d4e", h - 1))]
            for s in range(4):
                tp = S.emit("V", lambda e, s=s: e.tensor_tensor(
                    prodb[:, 0:512], qc[:, HLO:TH], kc_[:, HLO - s:TH - s],
                    op=Op.mult), deps=[tsk, tq, tdm] +
                    ([hs.get(("vacc", h - 1, 3))] if s == 0 else []))
                tdm = S.emit("T", lambda e, s=s: MM(
                    e, PB[4][:, 0:512], ohB[:, 128 * s:128 * s + 128],
                    prodb[:, 0:512], s == 0, s == 3),
                    deps=[tp, tok_g0] + d4dep)
            tce = S.emit("A", lambda e: e.activation(
                D4e, PB[4][:, 0:512], AF.Copy),
                deps=[tdm, hs.get(("coef", h - 1)),
                      hs.get(("lastst", h - 1)), hs[("rsq", h)]])
            hs[("d4e", h)] = tce
            hs[("d4f", h)] = tce
            tcf = S.emit("V", lambda e: e.tensor_tensor(
                D4e, D4e, RQ4, op=Op.mult), deps=[tce, tr2])
            tcf = S.emit("V", lambda e: e.tensor_tensor(
                D4e, D4e, RK4, op=Op.mult), deps=[tcf])
            tcf = S.emit("V", lambda e: e.tensor_tensor(
                D4e, D4e, G4, op=Op.mult), deps=[tcf] + tgk)
            hs[("coef", h)] = tcf
            outm = pf(O_OUT + h * 512, 512)
            for s in range(4):
                cb = PB[2 + (s % 2)]
                tst = S.emit("V", lambda e, s=s: e.tensor_copy(
                    crow2[0:1, 0:512], D4e[32 * s:32 * s + 1, 0:512]),
                    deps=[tcf] + ([hs.get(("bc", h, s - 1))] if s > 0 else
                                  [hs.get(("bc", h - 1, 3))]))
                tbc = S.emit("T", lambda e, cb=cb: MM(
                    e, cb[:, 0:512], ones_r[:], crow2[0:1, 0:512], True,
                    True), deps=[tst, hs.get(("vacc", h, s - 2)) if s >= 2
                                 else hs.get(("vacc", h - 1, s + 2))])
                hs[("bc", h, s)] = tbc
                if s == 0:
                    tva = S.emit("V", lambda e, cb=cb, outm=outm:
                                 e.tensor_tensor(outm, cb[:, 0:512],
                                                 vpost[:, HLO:TH],
                                                 op=Op.mult),
                                 deps=[tbc, tv_])
                else:
                    tva = S.emit("V", lambda e, cb=cb, s=s: e.tensor_tensor(
                        prodb[:, 0:512], cb[:, 0:512],
                        vpost[:, HLO - s:TH - s],
                        op=Op.mult), deps=[tbc, tv_])
                    tva = S.emit("V", lambda e, outm=outm: e.tensor_tensor(
                        outm, outm, prodb[:, 0:512], op=Op.add), deps=[tva])
                hs[("vacc", h, s)] = tva
                hs[("lastst", h)] = tst
            if tile == 0 and h == 0:
                dump("qc0", qc, hs[("coef", 0)], [128, 520])
                dump("kc0", kc_, hs[("coef", 0)], [128, 520])
                dump("rqrk0", pf(O_ROW + 512, 1024), hs[("rsq", 0)],
                     [128, 1024])
                dump("g40", G4, hs[("gex", 0)], [128, 512])
                dump("d4e0", D4e, hs[("coef", 0)], [128, 512])
                dump("out0", pf(O_OUT, 512), hs[("vacc", 0, 3)], [128, 512])

        gb = (N_GAM // H + 1) if tile == 0 else 0
        for h in range(H):
            front_mm(h, 0)
            front_mm(h, 1)
            gamma_step(gb)
            if h > 0:
                head_back(h - 1)
            front_evac(h, 0)
            front_evac(h, 1)
            front_mm(h, 2)
            front_evac(h, 2)
        head_back(H - 1)
        tok_mix_v = S.tok("v")
        tok_mix_a = S.tok("a")
        tok_mix_t = S.tok("t")

        if tile == 0:
            gamma_step(N_GAM)
            tga = S.tok("a")
            rs = []
            for wi, (a0, b0) in enumerate(((0, 88), (88, 176), (176, N_GAM))):
                rs.append(S.emit("V", lambda e, wi=wi, a0=a0, b0=b0:
                                 e.reduce_sum(pf(GACC + 280, 8)[:, wi:wi + 1],
                                              pf(GACC + a0, b0 - a0),
                                              axis=AX.X), deps=[tga]))
            tsum = S.emit("T", lambda e: MM(
                e, PB[7][0:1, 0:3], ones_c[:],
                pf(GACC + 280, 8)[:, 0:3], True, True),
                deps=rs + [tok_mix_a, tok_mix_v])
            tth = S.emit("A", lambda e: e.activation(
                grow[0:1, 0:3], PB[7][0:1, 0:3], AF.Copy, scale=0.5 / NW,
                bias=0.5e-8), deps=[tsum])
            tgc = S.emit("T", lambda e: MM(
                e, PB[2][:, 0:3], ones_r[:], grow[0:1, 0:3], True, True),
                deps=[tth, tok_mix_v])
            tgc = S.emit("A", lambda e: e.activation(
                gam_c[:, 0:3], PB[2][:, 0:3], AF.Copy), deps=[tgc])
            tok_gam = S.emit("V", lambda e: e.tensor_scalar(
                negc[:, 0:3], gam_c[:, 0:3], -1.0, None, op0=Op.mult),
                deps=[tgc])
            build_nc.tok_gam = tok_gam
        else:
            tok_gam = build_nc.tok_gam

        sqb = [pf(O_SQ, 520), pf(O_AB + 520, 520)]
        lastq = [None, None]
        for ch in range(KC):
            sq = sqb[ch % 2]
            tsq = S.emit("V", lambda e, ch=ch, sq=sq: e.tensor_tensor(
                sq[:, 0:T], pf(O_OUT + ch * 512, 512),
                pf(O_OUT + ch * 512, 512), op=Op.mult),
                deps=([tok_mix_v] if ch == 0 else []) + [lastq[ch % 2]])
            lastq[ch % 2] = S.emit("T", lambda e, ch=ch, sq=sq: MM(
                e, PB[4][0:1, 0:512], ones_c[:], sq[:, 0:T], ch == 0,
                ch == KC - 1), deps=[tsq, tok_mix_a] if ch == 0 else [tsq])
        rmu = pf(O_LR, 520)[0:1, 0:T]
        rmsc = pf(O_LR + 520, 520)[0:1, 0:T]
        trow = S.emit("V", lambda e: e.tensor_scalar(
            rmu, PB[4][0:1, 0:512], 1.0 / DM, 1.0e-6, op0=Op.mult,
            op1=Op.add), deps=[S.tok("t")])
        tq0 = S.emit("A", lambda e: e.activation(rmsc, rmu, AF.Ln),
                     deps=[trow])
        tq0 = S.emit("A", lambda e: e.activation(rmsc, rmsc, AF.Exp,
                                                 scale=-0.5), deps=[tq0])
        tn = S.emit("V", lambda e: e.tensor_tensor(
            pf(O_LR + 1040, 520)[0:1, 0:T], rmsc, rmsc, op=Op.mult),
            deps=[tq0])
        tn = S.emit("V", lambda e: e.tensor_tensor(
            pf(O_LR + 1040, 520)[0:1, 0:T], pf(O_LR + 1040, 520)[0:1, 0:T],
            rmu, op=Op.mult), deps=[tn])
        tn = S.emit("V", lambda e: e.tensor_scalar(
            pf(O_LR + 1040, 520)[0:1, 0:T], pf(O_LR + 1040, 520)[0:1, 0:T],
            -0.5, 1.5, op0=Op.mult, op1=Op.add), deps=[tn])
        trow = S.emit("V", lambda e: e.tensor_tensor(
            rmu, rmsc, pf(O_LR + 1040, 520)[0:1, 0:T], op=Op.mult),
            deps=[tn])
        trb = S.emit("T", lambda e: MM(
            e, PB[2][:, 0:512], ones_r[:], pf(O_LR, 520)[0:1, 0:T], True,
            True), deps=[trow, tok_mix_v])
        if tile == 0:
            dump("outF", pf(O_OUT, 8192), tok_mix_v, [128, 8192])
            dump("rms", pf(O_LR, 520), trow, [128, 520])
            dump("gamc", gam_c[:], tok_gam, [128, 3])
        zsil = pf(O_ZS, 512)
        gtmp = pf(O_ZS + 512, 512)
        wstate["split"] = tok_mix_t
        wstate["mmfree"] = [tok_mix_t, tok_mix_t]
        tok_g_done = []
        for m in range(KC):
            whi, wlo, wdeps = load_w_slab(d_wz, m, m % 2)
            zp = PB[m % 2]
            tl2 = None
            for ch in range(KC):
                whic = whi[:, ch * 128:(ch + 1) * 128]
                wloc = wlo[:, ch * 128:(ch + 1) * 128]
                tl2 = mm3(zp[:, 0:512], whic, wloc, ch, HLO, 512, ch == 0,
                          ch == KC - 1,
                          deps=(wdeps + [tok_mix_v,
                                         tok_g_done[m - 2] if m >= 2 else
                                         None]) if ch == 0 else [])
            wstate["mmfree"][m % 2] = tl2
            tz = S.emit("A", lambda e, zp=zp: e.activation(
                zsil, zp[:, 0:512], AF.Silu),
                deps=[tl2, tok_g_done[m - 1] if m >= 1 else None])
            tg1 = S.emit("V", lambda e, m=m: e.tensor_tensor(
                gtmp, pf(O_OUT + m * 512, 512), PB[2][:, 0:512], op=Op.mult),
                deps=[trb, tz])
            tg1 = S.emit("V", lambda e: e.tensor_tensor(
                gtmp, gtmp, zsil, op=Op.mult), deps=[tg1])
            tg1 = S.emit("V", lambda e, m=m: e.tensor_scalar(
                gtmp, gtmp, nwt[:, m:m + 1], None, op0=Op.mult),
                deps=[tg1])
            ghi = pbf(O_OUT + m * 512, 512)
            glo = pool_bf[:, 2 * (O_OUT + m * 512) + 512:
                          2 * (O_OUT + m * 512) + 1024]
            th2 = S.emit("A", lambda e, ghi=ghi: e.activation(
                ghi, gtmp, AF.Copy), deps=[tg1])
            tok_g_done.append(S.emit("V", lambda e, glo=glo, ghi=ghi:
                                     e.tensor_tensor(glo, gtmp, ghi,
                                                     op=Op.subtract),
                                     deps=[th2]))

        def g_hi(ch):
            return pbf(O_OUT + ch * 512, 512)

        def g_lo(ch):
            return pool_bf[:, 2 * (O_OUT + ch * 512) + 512:
                           2 * (O_OUT + ch * 512) + 1024]

        tok_ao = []
        for m in range(KC):
            whi, wlo, wdeps = load_w_slab(d_wo, m, m % 2)
            ap_ = PB[m % 2]
            tl2 = None
            for ch in range(KC):
                whic = whi[:, ch * 128:(ch + 1) * 128]
                wloc = wlo[:, ch * 128:(ch + 1) * 128]
                deps0 = (wdeps + [tok_g_done[-1],
                                  tok_ao[m - 2] if m >= 2 else None]) \
                    if ch == 0 else []
                S.emit("T", lambda e, ap_=ap_, whic=whic, ch=ch: MM(
                    e, ap_[:, 0:512], whic, g_hi(ch), ch == 0, False),
                    deps=deps0)
                S.emit("T", lambda e, ap_=ap_, whic=whic, ch=ch: MM(
                    e, ap_[:, 0:512], whic, g_lo(ch), False, False))
                tl2 = S.emit("T", lambda e, ap_=ap_, wloc=wloc, ch=ch: MM(
                    e, ap_[:, 0:512], wloc, g_hi(ch), False, ch == KC - 1))
            wstate["mmfree"][m % 2] = tl2
            tok_ao.append(S.emit("A", lambda e, m=m, ap_=ap_: e.activation(
                ao_ch(m), ap_[:, 0:512], AF.Copy), deps=[tl2]))

        if tile == 0:
            dump("ao0", ao_ch(0), tok_ao[0], [128, T])
        tok_res1 = resid_ln(False, [tok_ao[-1]])
        if tile == 0:
            dump("xr1", xt_ch(0), tok_res1, [128, TH])

        tok_h2_v, tok_h2_a = layernorm_to(
            False, [tok_res1], lambda ch: h_hi(ch, HLO, 512),
            lambda ch: h_lo(ch, HLO, 512))

        sigb = pf(F_SC, 512)
        sq2 = pf(F_SC + 512, 512)
        tmpb = pbf(F_SC, 2048)
        fst = {"d1": None, "d2": None, "q1mm": [None, None],
               "q2mm": [None, None], "tmpfree": None, "hidfree": None}

        def hid_hi(k):
            return pool_bf[:, 2 * (F_HID + k * 512): 2 * (F_HID + k * 512) + 512]

        def hid_lo(k):
            return pool_bf[:, 2 * (F_HID + k * 512) + 512:
                           2 * (F_HID + k * 512) + 1024]

        tok_hid = []
        tok_sg = []
        for m in range(MC):
            par = m % 2
            stg1 = pf(F_ST, 2048)
            stg2 = pf(F_ST + 2048, 2048)
            q1 = pbf(F_Q + par * 2048, 2048)
            q2 = pbf(F_Q + par * 2048 + 1024, 2048)
            d1 = S.emit("S", lambda e, m=m, stg1=stg1: e.dma_start(
                out=stg1, in_=d_w1[m * 128:(m + 1) * 128, :]),
                deps=[fst["d1"]] + ([tok_ao[-1], tok_h2_v, tok_h2_a]
                                    if m == 0 else []), dma="sd")
            d2 = S.emit("S", lambda e, m=m, stg2=stg2: e.dma_start(
                out=stg2, in_=d_w2[m * 128:(m + 1) * 128, :]),
                deps=[fst["d2"]] + ([tok_h2_v] if m == 0 else []), dma="sd")
            ts1 = S.emit("A", lambda e, q1=q1, stg1=stg1: e.activation(
                q1, stg1, AF.Sign, bias=negc[:, 0:1]),
                deps=[d1, tok_gam, fst["q1mm"][par], tok_h2_v])
            ts2 = S.emit("A", lambda e, stg1=stg1: e.activation(
                tmpb, stg1, AF.Sign, bias=gam_c[:, 0:1]),
                deps=[ts1, fst["tmpfree"]])
            tq1 = S.emit("V", lambda e, q1=q1: e.tensor_tensor(
                q1, q1, tmpb, op=Op.add), deps=[ts2])
            ts1b = S.emit("A", lambda e, q2=q2, stg2=stg2: e.activation(
                q2, stg2, AF.Sign, bias=negc[:, 1:2]),
                deps=[d2, tok_gam, fst["q2mm"][par]])
            ts2b = S.emit("A", lambda e, stg2=stg2: e.activation(
                tmpb, stg2, AF.Sign, bias=gam_c[:, 1:2]),
                deps=[ts1b, tq1])
            tq2 = S.emit("V", lambda e, q2=q2: e.tensor_tensor(
                q2, q2, tmpb, op=Op.add), deps=[ts2b])
            fst["d1"] = tq1
            fst["d2"] = tq2
            x1p = PB[par]
            x2p = PB[2 + par]
            tlast1 = None
            tlast2 = None
            for ch in range(KC):
                q1c = q1[:, ch * 128:(ch + 1) * 128]
                deps0 = [tq1, tok_h2_v, tok_h2_a,
                         tok_sg[m - 2] if m >= 2 else None] if ch == 0 else []
                S.emit("T", lambda e, x1p=x1p, q1c=q1c, ch=ch: MM(
                    e, x1p[:, 0:512], q1c, h_hi(ch, HLO, 512), ch == 0,
                    False), deps=deps0)
                tlast1 = S.emit("T", lambda e, x1p=x1p, q1c=q1c, ch=ch: MM(
                    e, x1p[:, 0:512], q1c, h_lo(ch, HLO, 512), False,
                    ch == KC - 1))
            for ch in range(KC):
                q2c = q2[:, ch * 128:(ch + 1) * 128]
                deps0 = [tq2, tok_hid[m - 2] if m >= 2 else None] \
                    if ch == 0 else []
                S.emit("T", lambda e, x2p=x2p, q2c=q2c, ch=ch: MM(
                    e, x2p[:, 0:512], q2c, h_hi(ch, HLO, 512), ch == 0,
                    False), deps=deps0)
                tlast2 = S.emit("T", lambda e, x2p=x2p, q2c=q2c, ch=ch: MM(
                    e, x2p[:, 0:512], q2c, h_lo(ch, HLO, 512), False,
                    ch == KC - 1))
            fst["q1mm"][par] = tlast1
            fst["q2mm"][par] = tlast2
            tsg = S.emit("A", lambda e, x1p=x1p: e.activation(
                sigb, x1p[:, 0:512], AF.Sigmoid, scale=0.5),
                deps=[tlast1, tq2, fst["hidfree"]])
            tok_sg.append(tsg)
            thd = S.emit("V", lambda e, x2p=x2p: e.scalar_tensor_tensor(
                sq2, x2p[:, 0:512], 0.5, sigb, op0=Op.mult, op1=Op.mult),
                deps=[tsg, tlast2])
            thh = S.emit("A", lambda e, m=m: e.activation(
                hid_hi(m), sq2, AF.Copy), deps=[thd])
            tlo = S.emit("V", lambda e, m=m: e.tensor_tensor(
                hid_lo(m), sq2, hid_hi(m), op=Op.subtract), deps=[thh])
            tok_hid.append(tlo)
            fst["tmpfree"] = tsg
            fst["hidfree"] = tlo
        tok_hid_all = S.tok("v")

        w3s = {"d": [None, None], "q": [None, None]}
        tok_ao2 = []
        for m in range(KC):
            ap2 = PB[4 + (m % 2)] if False else PB[m % 2]
            tl2 = None
            for half in range(4):
                p2 = half % 2
                k0 = half * 11
                stg = pf(W3_ST + p2 * 1408, 1408)
                q3 = pbf(W3_Q + p2 * 704, 1408)
                d3 = S.emit("S", lambda e, m=m, k0=k0, stg=stg: e.dma_start(
                    out=stg, in_=d_w3[m * 128:(m + 1) * 128,
                                      k0 * 128:(k0 + 11) * 128]),
                    deps=[w3s["d"][p2]] + ([tok_hid_all, tok_sg[-1]]
                                           if m == 0 and half == 0 else []),
                    dma="sd")
                tsa = S.emit("A", lambda e, q3=q3, stg=stg: e.activation(
                    q3, stg, AF.Sign, bias=negc[:, 2:3]),
                    deps=[d3, tok_gam, w3s["q"][p2]])
                tsb = S.emit("A", lambda e, stg=stg: e.activation(
                    tmpb[:, 0:1408], stg, AF.Sign, bias=gam_c[:, 2:3]),
                    deps=[tsa, tok_sg[-1] if m == 0 and half == 0 else None])
                tq3 = S.emit("V", lambda e, q3=q3: e.tensor_tensor(
                    q3, q3, tmpb[:, 0:1408], op=Op.add), deps=[tsb])
                w3s["d"][p2] = tq3
                for kk in range(11):
                    k = k0 + kk
                    q3c = q3[:, kk * 128:(kk + 1) * 128]
                    deps0 = [tq3] + ([tok_ao2[m - 2]] if m >= 2 and k == 0
                                     else [])
                    S.emit("T", lambda e, ap2=ap2, q3c=q3c, k=k: MM(
                        e, ap2[:, 0:512], q3c, hid_hi(k), k == 0, False),
                        deps=deps0)
                    tl2 = S.emit("T", lambda e, ap2=ap2, q3c=q3c, k=k: MM(
                        e, ap2[:, 0:512], q3c, hid_lo(k), False,
                        k == MC - 1))
                w3s["q"][p2] = tl2
            tok_ao2.append(S.emit("A", lambda e, m=m, ap2=ap2: e.activation(
                ao_ch(m), ap2[:, 0:512], AF.Copy), deps=[tl2]))

        if tile == 0:
            dump("hid0", pool_bf[:, 0:1024], tok_hid[0], [128, 1024], BF16)
            dump("ao20", ao_ch(0), tok_ao2[0], [128, T])
        tok_res2 = resid_ln(True, [tok_ao2[-1]])
        if tile == 0:
            dump("xf0", xt_ch(0), tok_res2, [128, TH])

        for ch in range(KC):
            src = xt_ch(ch, HLO, T)
            dst = d_out[ch * 128:(ch + 1) * 128, tile * T:(tile + 1) * T]
            S.emit("G", lambda e, src=src, dst=dst: e.dma_start(
                out=dst, in_=src), deps=[tok_res2], dma="gd")
        barrier = [S.tok("t"), S.tok("v"), S.tok("a"), S.tok("sd"),
                   S.tok("gd")]
        S.flush(blk)

    S.emit("G", lambda e: e.memset(fin[:], 1.0), deps=[S.tok("gd")])
    S.flush(blk)

    es.close()
    return nc


def _stage_kxm(w, kc, mc):
    return np.ascontiguousarray(
        np.asarray(w, np.float32).reshape(kc, 128, mc, 128)
        .transpose(2, 1, 0, 3).reshape(mc * 128, kc * 128))


def kernel(L0_attn_x, L0_attn_Wq, L0_attn_Wk, L0_attn_Wv, L0_attn_Wz,
           L0_attn_Wa, L0_attn_Wb, L0_attn_conv_q_w, L0_attn_conv_k_w,
           L0_attn_conv_v_w, L0_attn_norm_w, L0_attn_Wout,
           L0_ffn_w1, L0_ffn_w2, L0_ffn_w3):
    x = np.ascontiguousarray(np.asarray(L0_attn_x, dtype=np.float32))
    B, Sq, _ = x.shape
    x2d = x.reshape(B * Sq, DM)

    wq = _stage_kxm(L0_attn_Wq, KC, KC)
    wk = _stage_kxm(L0_attn_Wk, KC, KC)
    wv = _stage_kxm(L0_attn_Wv, KC, KC)
    wz = _stage_kxm(L0_attn_Wz, KC, KC)
    wo = _stage_kxm(L0_attn_Wout, KC, KC)
    w1t = _stage_kxm(np.ascontiguousarray(np.asarray(L0_ffn_w1, np.float32).T),
                     KC, MC)
    w2t = _stage_kxm(np.ascontiguousarray(np.asarray(L0_ffn_w2, np.float32).T),
                     KC, MC)
    w3t = _stage_kxm(np.ascontiguousarray(np.asarray(L0_ffn_w3, np.float32).T),
                     MC, KC)
    wa = np.asarray(L0_attn_Wa, np.float32).reshape(KC, 128, H).transpose(
        1, 0, 2).reshape(128, KC * H)
    wb = np.asarray(L0_attn_Wb, np.float32).reshape(KC, 128, H).transpose(
        1, 0, 2).reshape(128, KC * H)
    wab = np.ascontiguousarray(np.concatenate([wa, wb], axis=1))
    cw = np.concatenate(
        [np.asarray(L0_attn_conv_q_w, np.float32).reshape(H, 128, 4),
         np.asarray(L0_attn_conv_k_w, np.float32).reshape(H, 128, 4),
         np.asarray(L0_attn_conv_v_w, np.float32).reshape(H, 128, 4)],
        axis=2)
    cw = np.ascontiguousarray(cw.transpose(1, 0, 2).reshape(128, H * 12))
    nw = np.ascontiguousarray(
        np.asarray(L0_attn_norm_w, np.float32).reshape(KC, 128).T)

    nc = build_nc()

    in_maps = []
    for c in range(8):
        b = c // 4
        s0 = (c % 4) * TOK
        xs = np.zeros((HLO + TOK, DM), np.float32)
        r0 = b * Sq + s0
        xs[HLO:] = x2d[r0:r0 + TOK]
        if s0 > 0:
            xs[:HLO] = x2d[r0 - HLO:r0]
        in_maps.append({
            "xt": np.ascontiguousarray(xs.T), "wq": wq, "wk": wk, "wv": wv,
            "wz": wz, "wo": wo, "wab": wab, "cw": cw, "nw": nw,
            "pos0": np.full((128, 1), float(s0 - HLO), np.float32),
            "w1t": w1t, "w2t": w2t, "w3t": w3t,
        })

    res = run_bass_kernel_spmd(nc, in_maps, core_ids=list(range(8)))

    out = np.empty((B, Sq, DM), np.float32)
    for c in range(8):
        b = c // 4
        s0 = (c % 4) * TOK
        out[b, s0:s0 + TOK] = res.results[c]["out"].T
    return out


# version 05
# speedup vs baseline: 1.0218x; 1.0218x over previous
import math
import contextlib
import numpy as np

import concourse.bass as bass
import concourse.mybir as mybir
from concourse.bass_utils import run_bass_kernel_spmd
from concourse.alu_op_type import AluOpType as Op

F32 = mybir.dt.float32
BF16 = mybir.dt.bfloat16
AF = mybir.ActivationFunctionType
AX = mybir.AxisListType

DM = 2048
H = 16
KC = 16
FH = 5632
MC = 44
T = 512
HLO = 6
TH = T + HLO
NTILE = 2
TOK = NTILE * T
LN1E6 = math.log(1.0e6)
TWO_PI = 2.0 * math.pi
NW = float(FH * DM)


class Sched:
    ENGS = ("T", "V", "A", "G", "S")

    def __init__(self, sems):
        self.sems = sems
        self.cnt = {k: 0 for k in sems}
        self.seen = {e: {k: 0 for k in sems} for e in self.ENGS}
        self.plan = {e: [] for e in self.ENGS}
        self.own = {"T": "t", "V": "v", "A": "a", "G": "g", "S": "s"}

    def emit(self, eng, fn, deps=(), dma=None, sync_self=False):
        sem = dma if dma else self.own[eng]
        step = 16 if dma else 1
        self.plan[eng].append(([d for d in deps if d is not None], fn, sem,
                               step, sync_self))
        self.cnt[sem] += step
        return (sem, self.cnt[sem])

    def tok(self, sem):
        return (sem, self.cnt[sem])

    def flush(self, block):
        decs = {"T": block.tensor, "V": block.vector, "A": block.scalar,
                "G": block.gpsimd, "S": block.sync}
        for eng in self.ENGS:
            ops = self.plan[eng]
            if not ops:
                continue
            self.plan[eng] = []
            seen = self.seen[eng]
            sems = self.sems
            own = self.own[eng]

            def body(e, ops=ops, seen=seen, own=own, sems=sems):
                for deps, fn, sem, step, sself in ops:
                    for ds, dv in deps:
                        if (ds != own or sself) and seen[ds] < dv:
                            e.wait_ge(sems[ds], dv)
                            seen[ds] = dv
                    fn(e).then_inc(sems[sem], step)
            decs[eng](body)


def build_nc(dbg=False):
    nc = bass.Bass("TRN2", target_bir_lowering=False)
    dbg_specs = []

    def mkdump(name, shape, dt=F32):
        ap = nc.dram_tensor("dbg_" + name, shape, dt,
                            kind="ExternalOutput").ap()
        dbg_specs.append(name)
        return ap

    d_xt = nc.dram_tensor("xt", [DM, HLO + TOK], F32, kind="ExternalInput").ap()
    d_wq = nc.dram_tensor("wq", [DM, DM], F32, kind="ExternalInput").ap()
    d_wk = nc.dram_tensor("wk", [DM, DM], F32, kind="ExternalInput").ap()
    d_wv = nc.dram_tensor("wv", [DM, DM], F32, kind="ExternalInput").ap()
    d_wz = nc.dram_tensor("wz", [DM, DM], F32, kind="ExternalInput").ap()
    d_wo = nc.dram_tensor("wo", [DM, DM], F32, kind="ExternalInput").ap()
    d_wab = nc.dram_tensor("wab", [128, 2 * KC * H], F32, kind="ExternalInput").ap()
    d_cw = nc.dram_tensor("cw", [128, H * 12], F32, kind="ExternalInput").ap()
    d_nw = nc.dram_tensor("nw", [128, KC], F32, kind="ExternalInput").ap()
    d_pos0 = nc.dram_tensor("pos0", [128, 1], F32, kind="ExternalInput").ap()
    d_w1 = nc.dram_tensor("w1t", [FH, DM], F32, kind="ExternalInput").ap()
    d_w2 = nc.dram_tensor("w2t", [FH, DM], F32, kind="ExternalInput").ap()
    d_w3 = nc.dram_tensor("w3t", [DM, FH], F32, kind="ExternalInput").ap()
    d_out = nc.dram_tensor("out", [DM, TOK], F32, kind="ExternalOutput").ap()

    es = contextlib.ExitStack()

    def sb(name, shape, dt=F32):
        return es.enter_context(nc.sbuf_tensor(name, shape, dt))

    xt = sb("xts", [128, KC * TH])
    slot = sb("slot", [128, KC * TH])
    slot_bf = slot[:].bitcast(BF16)

    POOLW = 31770
    pool = sb("pool", [128, POOLW])
    pool_bf = pool[:].bitcast(BF16)

    ones_c = sb("ones_c", [128, 1])
    ones_r = sb("ones_r", [1, 128])
    half_c = sb("half_c", [128, 1])
    quar_c = sb("quar_c", [128, 1])
    iot_p = sb("iot_p", [128, 1])
    inv_c = sb("inv_c", [128, 1])
    pos0 = sb("pos0s", [128, 1])
    nwt = sb("nwt", [128, KC])
    cwt = sb("cwt", [128, H * 12])
    wab = sb("wabs", [128, 2 * KC * H])
    wab_h = sb("wab_h", [128, 2 * KC * H], BF16)
    wab_l = sb("wab_l", [128, 2 * KC * H], BF16)
    gam_c = sb("gam_c", [128, 3])
    negc = sb("negc", [128, 3])
    grow = sb("grow", [1, 8])
    ohB = sb("ohB", [128, 512])
    eye16 = sb("eye16", [16, 16])
    i16 = sb("i16", [16, 16])
    epsc = sb("epsc", [128, 3])
    grow2 = sb("grow2", [128, 3])
    fin = sb("fin", [1, 1])

    O_OUT = 0
    O_QKV = 8192
    O_WST = O_QKV + 3640
    O_WBF = O_WST + 2048
    O_GAM = O_WBF + 4096
    GACC = O_GAM + 2048
    O_G64 = GACC + 288
    O_ROW = O_G64 + 2048
    O_CROW = O_ROW + 2560
    O_SQ = O_CROW + 520
    O_CS = O_SQ + 520
    O_AB = O_CS + 1040
    O_LR = O_AB + 1040
    ATTN_END = O_LR + 2080
    assert ATTN_END <= POOLW, ATTN_END
    O_ZS = O_ROW + 1536
    O_GT = O_ROW + 2048

    F_HID = 0
    F_ST = 22528
    F_Q = F_ST + 4096
    F_SC = F_Q + 4096
    FFN_END = F_SC + 1024
    assert FFN_END <= POOLW, FFN_END
    W3_ST = F_ST
    W3_Q = F_ST + 2816

    def pf(off, n):
        return pool[:, off:off + n]

    def pbf(off_f32, n_bf16):
        return pool_bf[:, 2 * off_f32: 2 * off_f32 + n_bf16]

    sems = {}
    for name in ("t", "v", "a", "g", "s", "sd", "gd"):
        sems[name] = es.enter_context(nc.semaphore(name))
    S = Sched(sems)
    S.extra = None

    _orig_emit = S.emit

    def _emit2(eng, fn, deps=(), dma=None, sync_self=False):
        dl = list(deps)
        if S.extra is not None:
            dl.append(S.extra)
        return _orig_emit(eng, fn, deps=dl, dma=dma, sync_self=sync_self)
    if dbg:
        S.emit = _emit2

    def dump(name, src, dep, shape, dt=F32):
        if not dbg:
            return
        d = mkdump(name, shape, dt)
        tok = _orig_emit("G", lambda e, d=d, src=src: e.dma_start(
            out=d, in_=src), deps=[x for x in [dep, S.extra] if x], dma="gd")
        S.extra = tok

    PB = [es.enter_context(nc.psum_tensor("pb%d" % i, [128, 512], F32))
          for i in range(8)]

    def MM(e, out, lhsT, rhs, start, stop):
        return e.matmul(out, lhsT, rhs, start=start, stop=stop,
                        skip_group_check=True)

    blk = es.enter_context(nc.Block())

    def h_hi(ch, c0=0, n=TH):
        b = 2 * ch * TH
        return slot_bf[:, b + c0: b + c0 + n]

    def h_lo(ch, c0=0, n=TH):
        b = 2 * ch * TH + TH
        return slot_bf[:, b + c0: b + c0 + n]

    def ao_ch(ch):
        return slot[:, ch * TH: ch * TH + T]

    def xt_ch(ch, c0=0, n=TH):
        return xt[:, ch * TH + c0: ch * TH + c0 + n]

    S.emit("G", lambda e: e.memset(ones_c[:], 1.0))
    S.emit("G", lambda e: e.memset(ones_r[:], 1.0))
    S.emit("G", lambda e: e.memset(half_c[:], 0.5))
    S.emit("G", lambda e: e.memset(quar_c[:], 0.25))
    tok_g0 = S.emit("G", lambda e: e.iota(
        iot_p[:], pattern=[[0, 1]], base=0, channel_multiplier=1,
        allow_small_or_imprecise_dtypes=True))
    t_ohb = S.emit("G", lambda e: e.memset(ohB[:], 0.0))
    for s4 in range(4):
        t_ohb = S.emit("G", lambda e, s4=s4: e.memset(
            ohB[:, 128 * s4 + 32 * s4: 128 * s4 + 32 * s4 + 1], 1.0),
            deps=[t_ohb], sync_self=True)
    tk16 = S.emit("G", lambda e: e.iota(
        i16[:], pattern=[[1, 16]], base=0, channel_multiplier=0,
        allow_small_or_imprecise_dtypes=True))
    S.emit("V", lambda e: e.tensor_scalar(
        eye16[:], i16[:], iot_p[0:16, 0:1], None, op0=Op.is_equal),
        deps=[tk16, tok_g0])
    S.emit("G", lambda e: e.memset(epsc[:, 0:1], 1.0e-5))
    S.emit("G", lambda e: e.memset(epsc[:, 1:2], 1.0e-6))
    S.emit("G", lambda e: e.memset(epsc[:, 2:3], 0.5e-8))
    S.emit("S", lambda e: e.dma_start(out=pos0[:], in_=d_pos0), dma="sd")
    S.emit("S", lambda e: e.dma_start(out=nwt[:], in_=d_nw), dma="sd")
    S.emit("S", lambda e: e.dma_start(out=cwt[:], in_=d_cw), dma="sd")
    tok_small = S.emit("S", lambda e: e.dma_start(out=wab[:], in_=d_wab),
                       dma="sd")
    th_ = S.emit("A", lambda e: e.activation(wab_h[:], wab[:], AF.Copy),
                 deps=[tok_small])
    tok_wab = S.emit("V", lambda e: e.tensor_tensor(
        wab_l[:], wab[:], wab_h[:], op=Op.subtract), deps=[th_])
    tge = S.emit("V", lambda e: e.tensor_scalar(
        grow[0:1, 0:1] if False else inv_c[:], iot_p[:], 64.0, None,
        op0=Op.is_ge), deps=[tok_g0])
    tiv = S.emit("V", lambda e: e.scalar_tensor_tensor(
        grow2[:, 0:1], inv_c[:], -64.0, iot_p[:], op0=Op.mult, op1=Op.add),
        deps=[tge], sync_self=True)
    tok_inv = S.emit("V", lambda e: e.memset(inv_c[:], 1.0), deps=[tiv],
                     sync_self=True)
    for j in range(5, -1, -1):
        p2 = float(1 << j)
        r2j = float(1.0e6 ** (-(1 << j) / 64.0))
        tb = S.emit("V", lambda e, p2=p2: e.tensor_scalar(
            grow2[:, 1:2], grow2[:, 0:1], p2, None, op0=Op.is_ge),
            deps=[tiv, tok_inv], sync_self=True)
        tiv = S.emit("V", lambda e, p2=p2: e.scalar_tensor_tensor(
            grow2[:, 0:1], grow2[:, 1:2], -p2, grow2[:, 0:1], op0=Op.mult,
            op1=Op.add), deps=[tb], sync_self=True)
        tf = S.emit("V", lambda e, r2j=r2j: e.tensor_scalar(
            grow2[:, 2:3], grow2[:, 1:2], r2j - 1.0, 1.0, op0=Op.mult,
            op1=Op.add), deps=[tiv], sync_self=True)
        tok_inv = S.emit("V", lambda e: e.tensor_tensor(
            inv_c[:], inv_c[:], grow2[:, 2:3], op=Op.mult), deps=[tf],
            sync_self=True)

    gam_jobs = []
    idx = 0
    for dram, rows in ((d_w1, FH), (d_w2, FH), (d_w3, DM)):
        cols = DM if rows == FH else FH
        for rc in range(rows // 128):
            for cs in range(0, cols, 1024):
                w = min(1024, cols - cs)
                gam_jobs.append((dram, rc * 128, cs, w, idx))
                idx += 1
    N_GAM = len(gam_jobs)
    GLIM = {"w1": 88, "w2": 176, "w3": N_GAM}
    assert N_GAM <= 288 - 16, N_GAM
    gst = {"i": 0, "last": [None, None]}

    def gamma_step(nmax):
        for _ in range(nmax):
            i = gst["i"]
            if i >= N_GAM:
                return
            gst["i"] = i + 1
            dram, r0, c0, w, col = gam_jobs[i]
            buf = pf(O_GAM + (i % 2) * 1024, w)
            td = S.emit("G", lambda e, dram=dram, r0=r0, c0=c0, w=w, buf=buf:
                        e.dma_start(out=buf, in_=dram[r0:r0 + 128, c0:c0 + w]),
                        deps=[gst["last"][i % 2]], dma="gd")
            gst["last"][i % 2] = S.emit(
                "A", lambda e, buf=buf, col=col: e.activation(
                    buf, buf, AF.Abs, accum_out=pf(GACC + col, 1)), deps=[td])

    wstate = {"split": None, "mmfree": [None, None]}

    def load_w_slab(dram, m, par, extra=()):
        stg = pf(O_WST, 2048)
        whi = pbf(O_WBF + par * 2048, 2048)
        wlo = pbf(O_WBF + par * 2048 + 1024, 2048)
        td = S.emit("S", lambda e, dram=dram, m=m, stg=stg: e.dma_start(
            out=stg, in_=dram[m * 128:(m + 1) * 128, :]),
            deps=[wstate["split"]] + list(extra), dma="sd")
        tH = S.emit("A", lambda e, whi=whi, stg=stg: e.activation(
            whi, stg, AF.Copy), deps=[td, wstate["mmfree"][par]])
        tL = S.emit("V", lambda e, wlo=wlo, stg=stg, whi=whi: e.tensor_tensor(
            wlo, stg, whi, op=Op.subtract),
            deps=[tH, wstate["mmfree"][par]])
        wstate["split"] = tL
        return whi, wlo, [tH, tL]

    def mm3(out, whi, wlo, ch, c0, n, start, stop, deps=()):
        S.emit("T", lambda e: MM(e, out, whi, h_hi(ch, c0, n), start, False),
               deps=deps)
        S.emit("T", lambda e: MM(e, out, whi, h_lo(ch, c0, n), False, False))
        return S.emit("T", lambda e: MM(e, out, wlo, h_hi(ch, c0, n), False,
                                        stop))

    def layernorm_to(grid_full, dep_first, out_hi, out_lo):
        c0 = 0 if grid_full else HLO
        n = TH if grid_full else T
        sqb = [pf(O_SQ, 520), pf(O_AB + 520, 520)]
        lastq = [None, None]
        for ch in range(KC):
            src = xt_ch(ch, c0, n)
            sq = sqb[ch % 2]
            S.emit("T", lambda e, src=src, ch=ch: MM(
                e, PB[4][0:1, 0:512], ones_c[:], src[:, 0:512],
                ch == 0, ch == KC - 1), deps=dep_first if ch == 0 else [])
            tq = S.emit("V", lambda e, src=src, n=n, sq=sq: e.tensor_tensor(
                sq[:, 0:n], src, src, op=Op.mult),
                deps=(dep_first if ch == 0 else []) + [lastq[ch % 2]])
            S.emit("T", lambda e, ch=ch, sq=sq: MM(
                e, PB[5][0:1, 0:512], ones_c[:], sq[:, 0:512],
                ch == 0, ch == KC - 1), deps=[tq])
            if grid_full:
                S.emit("T", lambda e, src=src, ch=ch: MM(
                    e, PB[6][0:1, 0:6], ones_c[:], src[:, 512:518],
                    ch == 0, ch == KC - 1))
                lastq[ch % 2] = S.emit("T", lambda e, ch=ch, sq=sq: MM(
                    e, PB[7][0:1, 0:6], ones_c[:], sq[:, 512:518],
                    ch == 0, ch == KC - 1), deps=[tq])
            else:
                lastq[ch % 2] = S.tok("t")
        tmm = S.tok("t")
        mu = pf(O_LR, 520)[0:1, 0:n]
        msq = pf(O_LR + 520, 520)[0:1, 0:n]
        var = pf(O_LR + 1040, 520)[0:1, 0:n]
        ist = pf(O_LR + 1560, 520)[0:1, 0:n]
        S.emit("A", lambda e, mu=mu: e.activation(
            mu[:, 0:512], PB[4][0:1, 0:512], AF.Copy, scale=1.0 / DM),
            deps=[tmm])
        ta = S.emit("A", lambda e, msq=msq: e.activation(
            msq[:, 0:512], PB[5][0:1, 0:512], AF.Copy, scale=1.0 / DM),
            deps=[tmm])
        if grid_full:
            tev1 = S.emit("A", lambda e, mu=mu: e.activation(
                mu[:, 512:518], PB[6][0:1, 0:6], AF.Copy, scale=1.0 / DM),
                deps=[tmm])
            ta = S.emit("A", lambda e, msq=msq: e.activation(
                msq[:, 512:518], PB[7][0:1, 0:6], AF.Copy, scale=1.0 / DM),
                deps=[tmm])
        tv = S.emit("V", lambda e, var=var, mu=mu: e.tensor_tensor(
            var, mu, mu, op=Op.mult), deps=[ta])
        tv = S.emit("V", lambda e, var=var, msq=msq: e.tensor_tensor(
            var, msq, var, op=Op.subtract), deps=[tv])
        tv = S.emit("V", lambda e, var=var: e.tensor_scalar(
            var, var, 1.0, 1.0e-5, op0=Op.mult, op1=Op.add), deps=[tv])
        tis = S.emit("A", lambda e, ist=ist, var=var: e.activation(
            ist, var, AF.Ln), deps=[tv])
        tis = S.emit("A", lambda e, ist=ist: e.activation(
            ist, ist, AF.Exp, scale=-0.5), deps=[tis])
        tis = S.emit("V", lambda e, ist=ist, msq=msq: e.tensor_tensor(
            msq, ist, ist, op=Op.mult), deps=[tis])
        tis = S.emit("V", lambda e, msq=msq, var=var: e.tensor_tensor(
            msq, msq, var, op=Op.mult), deps=[tis])
        tis = S.emit("V", lambda e, msq=msq: e.tensor_scalar(
            msq, msq, -0.5, 1.5, op0=Op.mult, op1=Op.add), deps=[tis])
        tis = S.emit("V", lambda e, ist=ist, msq=msq: e.tensor_tensor(
            ist, ist, msq, op=Op.mult), deps=[tis])
        S.emit("T", lambda e, mu=mu: MM(
            e, PB[2][:, 0:512], ones_r[:], mu[:, 0:512], True, True),
            deps=[tis])
        S.emit("T", lambda e, ist=ist: MM(
            e, PB[3][:, 0:512], ones_r[:], ist[:, 0:512], True, True),
            deps=[tis])
        if grid_full:
            S.emit("T", lambda e, mu=mu: MM(
                e, PB[6][:, 0:6], ones_r[:], mu[:, 512:518], True, True),
                deps=[tis])
            S.emit("T", lambda e, ist=ist: MM(
                e, PB[7][:, 0:6], ones_r[:], ist[:, 512:518], True, True),
                deps=[tis])
        tbc = S.tok("t")
        scr = pf(O_SQ, n)
        for ch in range(KC):
            src = xt_ch(ch, c0, n)
            tm = S.emit("V", lambda e, src=src: e.tensor_tensor(
                scr[:, 0:512], src[:, 0:512], PB[2][:, 0:512],
                op=Op.subtract), deps=[tbc])
            tm = S.emit("V", lambda e: e.tensor_tensor(
                scr[:, 0:512], scr[:, 0:512], PB[3][:, 0:512], op=Op.mult),
                deps=[tm])
            if grid_full:
                tm2 = S.emit("V", lambda e, src=src: e.tensor_tensor(
                    scr[:, 512:518], src[:, 512:518], PB[6][:, 0:6],
                    op=Op.subtract), deps=[tbc])
                tm = S.emit("V", lambda e: e.tensor_tensor(
                    scr[:, 512:518], scr[:, 512:518], PB[7][:, 0:6],
                    op=Op.mult), deps=[tm2, tm], sync_self=True)
            thc = S.emit("A", lambda e, ch=ch: e.activation(
                out_hi(ch), scr, AF.Copy), deps=[tm])
            S.emit("V", lambda e, ch=ch: e.tensor_tensor(
                out_lo(ch), scr, out_hi(ch), op=Op.subtract), deps=[thc])
        return S.tok("v"), S.tok("a")

    def resid_ln(halfed, dep_first):
        oc = half_c if halfed else ones_c
        oq = quar_c if halfed else ones_c
        sqb = [pf(O_SQ, 520), pf(O_AB + 520, 520)]
        lastq = [None, None]
        for ch in range(KC):
            src = ao_ch(ch)
            sq = sqb[ch % 2]
            S.emit("T", lambda e, src=src, ch=ch, oc=oc: MM(
                e, PB[4][0:1, 0:512], oc[:], src, ch == 0, ch == KC - 1),
                deps=dep_first if ch == 0 else [])
            tq = S.emit("V", lambda e, src=src, sq=sq: e.tensor_tensor(
                sq[:, 0:T], src, src, op=Op.mult),
                deps=(dep_first if ch == 0 else []) + [lastq[ch % 2]])
            lastq[ch % 2] = S.emit("T", lambda e, ch=ch, oq=oq, sq=sq: MM(
                e, PB[5][0:1, 0:512], oq[:], sq[:, 0:T], ch == 0,
                ch == KC - 1), deps=[tq])
        tmm = S.tok("t")
        mu = pf(O_LR, 520)[0:1, 0:T]
        msq = pf(O_LR + 520, 520)[0:1, 0:T]
        var = pf(O_LR + 1040, 520)[0:1, 0:T]
        ist = pf(O_LR + 1560, 520)[0:1, 0:T]
        S.emit("A", lambda e, mu=mu: e.activation(
            mu, PB[4][0:1, 0:512], AF.Copy, scale=1.0 / DM), deps=[tmm])
        ta = S.emit("A", lambda e, msq=msq: e.activation(
            msq, PB[5][0:1, 0:512], AF.Copy, scale=1.0 / DM), deps=[tmm])
        tv = S.emit("V", lambda e, var=var, mu=mu: e.tensor_tensor(
            var, mu, mu, op=Op.mult), deps=[ta])
        tv = S.emit("V", lambda e, var=var, msq=msq: e.tensor_tensor(
            var, msq, var, op=Op.subtract), deps=[tv])
        tv = S.emit("V", lambda e, var=var: e.tensor_scalar(
            var, var, 1.0, 1.0e-5, op0=Op.mult, op1=Op.add), deps=[tv])
        tis = S.emit("A", lambda e, ist=ist, var=var: e.activation(
            ist, var, AF.Ln), deps=[tv])
        tis = S.emit("A", lambda e, ist=ist: e.activation(
            ist, ist, AF.Exp, scale=-0.5), deps=[tis])
        tis = S.emit("V", lambda e, ist=ist, msq=msq: e.tensor_tensor(
            msq, ist, ist, op=Op.mult), deps=[tis])
        tis = S.emit("V", lambda e, msq=msq, var=var: e.tensor_tensor(
            msq, msq, var, op=Op.mult), deps=[tis])
        tis = S.emit("V", lambda e, msq=msq: e.tensor_scalar(
            msq, msq, -0.5, 1.5, op0=Op.mult, op1=Op.add), deps=[tis])
        tis = S.emit("V", lambda e, ist=ist, msq=msq: e.tensor_tensor(
            ist, ist, msq, op=Op.mult), deps=[tis])
        S.emit("T", lambda e, mu=mu: MM(
            e, PB[2][:, 0:512], ones_r[:], mu, True, True), deps=[tis])
        S.emit("T", lambda e, ist=ist: MM(
            e, PB[3][:, 0:512], ones_r[:], ist, True, True), deps=[tis])
        tbc = S.tok("t")
        lastv = None
        for ch in range(KC):
            src = ao_ch(ch)
            xdst = xt_ch(ch, HLO, T)
            if halfed:
                tm = S.emit("V", lambda e, src=src: e.scalar_tensor_tensor(
                    pf(O_SQ, T), src, 0.5, PB[2][:, 0:512], op0=Op.mult,
                    op1=Op.subtract), deps=[tbc])
            else:
                tm = S.emit("V", lambda e, src=src: e.tensor_tensor(
                    pf(O_SQ, T), src, PB[2][:, 0:512], op=Op.subtract),
                    deps=[tbc])
            tm = S.emit("V", lambda e: e.tensor_tensor(
                pf(O_SQ, T), pf(O_SQ, T), PB[3][:, 0:512], op=Op.mult),
                deps=[tm])
            lastv = S.emit("V", lambda e, xdst=xdst: e.tensor_tensor(
                xdst, xdst, pf(O_SQ, T), op=Op.add), deps=[tm])
        return lastv

    barrier = []
    for tile in range(NTILE):
        for ch in range(KC):
            src = d_xt[ch * 128:(ch + 1) * 128, tile * T: tile * T + TH]
            S.emit("G", lambda e, ch=ch, src=src: e.dma_start(
                out=xt_ch(ch), in_=src),
                deps=barrier if ch == 0 else [], dma="gd")
        tok_xload = S.tok("gd")

        cosb = pf(O_CS, TH)
        sinb = pf(O_CS + 520, TH)
        u = pf(O_AB, TH)
        uscr = pf(O_AB + 520, TH)
        t0 = S.emit("G", lambda e: e.iota(
            u, pattern=[[1, TH]], base=0, channel_multiplier=0,
            allow_small_or_imprecise_dtypes=True), deps=barrier)
        tl_ = S.emit("V", lambda e, tile=tile: e.tensor_scalar(
            u, u, pos0[:, 0:1], float(tile * T), op0=Op.add, op1=Op.add),
            deps=[t0, tok_small] + barrier)
        tl_ = S.emit("V", lambda e: e.tensor_scalar(
            u, u, inv_c[:, 0:1], None, op0=Op.mult), deps=[tl_, tok_inv])
        tl_ = S.emit("V", lambda e: e.tensor_scalar(
            u, u, 1.0 / TWO_PI, 1024.0, op0=Op.mult, op1=Op.add), deps=[tl_])
        for j in range(10, -1, -1):
            p2 = float(1 << j)
            tl_ = S.emit("V", lambda e, p2=p2: e.tensor_scalar(
                uscr, u, p2, None, op0=Op.is_ge), deps=[tl_])
            tl_ = S.emit("V", lambda e, p2=p2: e.scalar_tensor_tensor(
                u, uscr, -p2, u, op0=Op.mult, op1=Op.add), deps=[tl_])
        tl_ = S.emit("V", lambda e: e.tensor_scalar(
            uscr, u, 0.5, None, op0=Op.is_ge), deps=[tl_])
        tl_ = S.emit("V", lambda e: e.scalar_tensor_tensor(
            u, uscr, -1.0, u, op0=Op.mult, op1=Op.add), deps=[tl_])
        if tile == 0:
            dump("invc", inv_c[:], tok_inv, [128, 1])
            dump("mrow", u, tl_, [128, TH])
        tsn = S.emit("A", lambda e: e.activation(sinb, u, AF.Sin,
                                                 scale=TWO_PI), deps=[tl_])
        tl_ = S.emit("V", lambda e: e.tensor_scalar(
            u, u, -1.0, 0.25, op0=Op.mult, op1=Op.add), deps=[tsn])
        tl_ = S.emit("V", lambda e: e.tensor_scalar(
            uscr, u, 0.5, None, op0=Op.is_ge), deps=[tl_])
        tl_ = S.emit("V", lambda e: e.scalar_tensor_tensor(
            u, uscr, -1.0, u, op0=Op.mult, op1=Op.add), deps=[tl_])
        tok_tab = S.emit("A", lambda e: e.activation(
            cosb, u, AF.Sin, scale=TWO_PI), deps=[tl_])
        if tile == 0:
            dump("cos", cosb, tok_tab, [128, TH])
            dump("sin", sinb, tok_tab, [128, TH])

        tok_h_v, tok_h_a = layernorm_to(
            True, [tok_xload] + barrier, lambda ch: h_hi(ch),
            lambda ch: h_lo(ch))
        if tile == 0:
            dump("xt0", xt_ch(0), tok_xload, [128, TH])
            dump("murow", pf(O_LR, 2080), tok_h_v, [128, 2080])
            dump("istd1", pf(O_LR + 1560, 518), tok_h_v, [128, 518])
            dump("hh0", h_hi(0), tok_h_v, [128, TH], BF16)
            dump("hl0", h_lo(0), tok_h_v, [128, TH], BF16)

        for wi in range(2):
            dst = PB[7][0:16, 0:512] if wi == 0 else PB[6][0:16, 0:512]
            for ch in range(KC):
                whi = wab_h[:, wi * KC * H + ch * H: wi * KC * H + (ch + 1) * H]
                wlo = wab_l[:, wi * KC * H + ch * H: wi * KC * H + (ch + 1) * H]
                st, sp = ch == 0, ch == KC - 1
                deps = [tok_wab, tok_h_v, tok_h_a] if ch == 0 else []
                S.emit("T", lambda e, dst=dst, whi=whi, ch=ch, st=st: MM(
                    e, dst, whi, h_hi(ch, HLO, 512), st, False), deps=deps)
                S.emit("T", lambda e, dst=dst, whi=whi, ch=ch: MM(
                    e, dst, whi, h_lo(ch, HLO, 512), False, False))
                S.emit("T", lambda e, dst=dst, wlo=wlo, ch=ch, sp=sp: MM(
                    e, dst, wlo, h_hi(ch, HLO, 512), False, sp))
                if wi == 0:
                    tdst = PB[5][0:16, 0:3]
                    S.emit("T", lambda e, tdst=tdst, whi=whi, ch=ch, st=st: MM(
                        e, tdst, whi, h_hi(ch, 3, 3), st, False))
                    S.emit("T", lambda e, tdst=tdst, whi=whi, ch=ch: MM(
                        e, tdst, whi, h_lo(ch, 3, 3), False, False))
                    S.emit("T", lambda e, tdst=tdst, wlo=wlo, ch=ch, sp=sp: MM(
                        e, tdst, wlo, h_hi(ch, 3, 3), False, sp))
        tabm = S.tok("t")
        abuf = pf(O_AB, 520)
        bbuf = pf(O_AB + 520, 520)
        S.emit("A", lambda e: e.activation(abuf[0:H, HLO:TH],
                                           PB[7][0:16, 0:512], AF.Copy),
               deps=[tabm, tok_tab])
        S.emit("A", lambda e: e.activation(abuf[0:H, 3:6], PB[5][0:16, 0:3],
                                           AF.Copy), deps=[tabm])
        tok_ab = S.emit("A", lambda e: e.activation(
            bbuf[0:H, HLO:TH], PB[6][0:16, 0:512], AF.Copy), deps=[tabm])

        g64 = pf(O_G64, 2048)
        tg = []
        for s in range(4):
            tg.append(S.emit("V", lambda e, s=s: e.tensor_tensor(
                g64[0:H, 512 * s:512 * s + 512], abuf[0:H, HLO - s:TH - s],
                bbuf[0:H, HLO:TH], op=Op.mult), deps=[tok_ab]))
        tok_g64 = S.emit("A", lambda e: e.activation(
            g64[0:H, :], g64[0:H, :], AF.Sigmoid), deps=tg)

        qb = pf(O_QKV, 520)
        kb = pf(O_QKV + 520, 520)
        vb = pf(O_QKV + 1040, 520)
        qc = pf(O_QKV + 1560, 520)
        kc_ = pf(O_QKV + 2080, 520)
        prodb = pf(O_QKV + 2600, 520)
        vpost = pf(O_QKV + 3120, 520)
        D4e = pf(O_ROW, 512)
        RQ4 = pf(O_ROW + 512, 512)
        RK4 = pf(O_ROW + 1024, 512)
        G4 = pf(O_ROW + 1536, 512)
        KT = pf(O_ROW + 2048, 8)
        crow2 = pf(O_CROW, 512)
        hs = {}
        wstate["split"] = S.tok("v")
        wstate["mmfree"] = [None, None]

        SPECS = [("q", None, 0, 0), ("k", None, 1, 1), ("v", None, 0, 0)]

        def front_mm(h, idx):
            nm, dram, par, bank = [("q", d_wq, 0, 0), ("k", d_wk, 1, 1),
                                   ("v", d_wv, 0, 0)][idx]
            if idx == 2:
                wstate["mmfree"][0] = hs[("mm", h, 0)]
            whi, wlo, wdeps = load_w_slab(
                dram, h, par, extra=[tok_h_a] if h == 0 and idx == 0 else [])
            psm = PB[bank]
            prev_evac = (hs.get(("ev", h, 0)) if idx == 2 else
                         hs.get(("ev", h - 1, 2)) if idx == 0 else
                         hs.get(("ev", h - 1, 1)))
            tl2 = None
            for ch in range(KC):
                whic = whi[:, ch * 128:(ch + 1) * 128]
                wloc = wlo[:, ch * 128:(ch + 1) * 128]
                tl2 = mm3(psm[:, 0:512], whic, wloc, ch, 0, 512,
                          ch == 0, ch == KC - 1,
                          deps=(wdeps + [tok_h_v, tok_h_a, prev_evac])
                          if ch == 0 else [])
            hs[("tmain", h, idx)] = tl2
            for ch in range(KC):
                whic = whi[:, ch * 128:(ch + 1) * 128]
                wloc = wlo[:, ch * 128:(ch + 1) * 128]
                tl2 = mm3(PB[5][:, idx * 8: idx * 8 + 6], whic, wloc,
                          ch, 512, 6, ch == 0, ch == KC - 1,
                          deps=[tok_ab] if h == 0 and idx == 0 and
                          ch == 0 else [])
            hs[("mm", h, idx)] = tl2
            if idx == 1:
                wstate["mmfree"][1] = tl2
            if idx == 2:
                wstate["mmfree"][0] = tl2

        def front_evac(h, idx):
            nm = ("q", "k", "v")[idx]
            buf = {"q": qb, "k": kb, "v": vb}[nm]
            psm = PB[(0, 1, 0)[idx]]
            freed = (hs.get(("useqk", h - 1)) if idx in (0, 1) else
                     hs.get(("usev", h - 1)))
            S.emit("A", lambda e, buf=buf, psm=psm: e.activation(
                buf[:, 0:512], psm[:, 0:512], AF.Silu),
                deps=[hs[("tmain", h, idx)], freed])
            hs[("ev", h, idx)] = S.emit(
                "A", lambda e, buf=buf, idx=idx: e.activation(
                    buf[:, 512:518], PB[5][:, idx * 8: idx * 8 + 6],
                    AF.Silu), deps=[hs[("mm", h, idx)], freed])
            if idx == 2:
                hs[("front", h)] = S.tok("a")

        def conv_rope(h, pre, post, rotbuf, cwcol, do_rope, dep0):
            tc = S.emit("V", lambda e, pre=pre, post=post, cwcol=cwcol:
                        e.tensor_scalar(post[:, 0:TH], pre[:, 0:TH],
                                        cwt[:, cwcol + 3:cwcol + 4], None,
                                        op0=Op.mult), deps=dep0)
            for j in range(3):
                sft = 3 - j
                tc = S.emit("V", lambda e, pre=pre, post=post, j=j, sft=sft,
                            cwcol=cwcol: e.scalar_tensor_tensor(
                    post[:, sft:TH], pre[:, 0:TH - sft],
                    cwt[:, cwcol + j:cwcol + j + 1], post[:, sft:TH],
                    op0=Op.mult, op1=Op.add), deps=[tc])
            if not do_rope:
                return tc
            tc = S.emit("V", lambda e, post=post, rotbuf=rotbuf:
                        e.tensor_scalar(rotbuf[0:64, 3:TH],
                                        post[64:128, 3:TH], -1.0, None,
                                        op0=Op.mult), deps=[tc])
            tc = S.emit("V", lambda e, post=post, rotbuf=rotbuf:
                        e.tensor_copy(rotbuf[64:128, 3:TH], post[0:64, 3:TH]),
                        deps=[tc])
            tc = S.emit("V", lambda e, post=post: e.tensor_tensor(
                post[:, 3:TH], post[:, 3:TH], cosb[:, 3:TH], op=Op.mult),
                deps=[tc, tok_tab])
            tc = S.emit("V", lambda e, rotbuf=rotbuf: e.tensor_tensor(
                rotbuf[:, 3:TH], rotbuf[:, 3:TH], sinb[:, 3:TH], op=Op.mult),
                deps=[tc])
            tc = S.emit("V", lambda e, post=post, rotbuf=rotbuf:
                        e.tensor_tensor(post[:, 3:TH], post[:, 3:TH],
                                        rotbuf[:, 3:TH], op=Op.add), deps=[tc])
            return tc

        def head_back(h):
            dep0 = [hs[("front", h)]]
            cwb = h * 12
            if tile == 0 and h == 0:
                dump("qpre0", qb, hs[("front", 0)], [128, 520])
                dump("kpre0", kb, hs[("front", 0)], [128, 520])
                dump("vpre0", vb, hs[("front", 0)], [128, 520])
            tq = conv_rope(h, qb, qc, prodb, cwb, True, dep0)
            tsq = S.emit("V", lambda e: e.tensor_tensor(
                pf(O_SQ, TH), qc[:, 0:TH], qc[:, 0:TH], op=Op.mult), deps=[tq])
            tl2q = S.emit("T", lambda e: MM(
                e, PB[6][0:1, 0:512], ones_c[:], pf(O_SQ + 6, 512), True,
                True), deps=[tsq, hs.get(("l2e", h - 1)),
                             hs.get(("gex", h - 1)),
                             tok_ab if h == 0 else None])
            tk = conv_rope(h, kb, kc_, qb, cwb + 4, True, [tq])
            tsk = S.emit("V", lambda e: e.tensor_tensor(
                pf(O_SQ, TH), kc_[:, 0:TH], kc_[:, 0:TH], op=Op.mult),
                deps=[tk, S.tok("t")])
            S.emit("T", lambda e: MM(
                e, PB[7][0:1, 0:512], ones_c[:], pf(O_SQ + 6, 512), True,
                True), deps=[tsk, hs.get(("l2e", h - 1)),
                             hs.get(("gex", h - 1)),
                             tok_ab if h == 0 else None])
            tl2k = S.emit("T", lambda e: MM(
                e, PB[5][0:1, 24:27], ones_c[:], pf(O_SQ + 3, 3), True,
                True), deps=[tsk, hs.get(("l2e", h - 1))])
            tv_ = conv_rope(h, vb, vpost, None, cwb + 8, False, [tk])
            hs[("useqk", h)] = tk
            hs[("usev", h)] = tv_
            rsqdep = [hs.get(("rsq", h - 1)), hs.get(("coef", h - 1))]
            te1 = S.emit("A", lambda e: e.activation(
                RQ4[0:1, 0:512], PB[6][0:1, 0:512], AF.Copy),
                deps=[tl2q] + rsqdep)
            te2 = S.emit("A", lambda e: e.activation(
                RK4[0:1, 0:512], PB[7][0:1, 0:512], AF.Copy),
                deps=[tl2k] + rsqdep)
            te4 = S.emit("A", lambda e: e.activation(
                KT[0:1, 0:3], PB[5][0:1, 24:27], AF.Copy), deps=[tl2k])
            hs[("l2e", h)] = S.tok("a")
            tq4 = []
            for s in range(1, 4):
                tq4.append(S.emit("V", lambda e, s=s: e.tensor_copy(
                    RQ4[32 * s:32 * s + 1, 0:512], RQ4[0:1, 0:512]),
                    deps=[te1] + rsqdep))
                tq4.append(S.emit("V", lambda e, s=s: e.tensor_copy(
                    RK4[32 * s:32 * s + 1, s:512], RK4[0:1, 0:512 - s]),
                    deps=[te2] + rsqdep))
                tq4.append(S.emit("V", lambda e, s=s: e.tensor_copy(
                    RK4[32 * s:32 * s + 1, 0:s], KT[0:1, 3 - s:3]),
                    deps=[te4] + rsqdep))
            tmx = S.emit("V", lambda e: e.tensor_scalar(
                pf(O_ROW + 512, 1024), pf(O_ROW + 512, 1024),
                1.0e-24, None, op0=Op.max), deps=tq4 + [te1, te2],
                sync_self=True)
            tr2 = tmx
            for blk_i, (ublk, fold) in enumerate(
                    ((RQ4, 1.0 / math.sqrt(128.0)), (RK4, 1.0))):
                ta1 = S.emit("A", lambda e, ublk=ublk: e.activation(
                    crow2[:, 0:512], ublk, AF.Ln),
                    deps=[tr2, hs.get(("lastst", h - 1)),
                          hs.get(("bc", h - 1, 3))])
                ta1 = S.emit("A", lambda e: e.activation(
                    D4e, crow2[:, 0:512], AF.Exp, scale=-0.5),
                    deps=[ta1, hs.get(("lastst", h - 1)),
                          hs.get(("d4f", h - 1))])
                tn = S.emit("V", lambda e: e.tensor_tensor(
                    crow2[:, 0:512], D4e, D4e, op=Op.mult), deps=[ta1])
                tn = S.emit("V", lambda e, ublk=ublk: e.tensor_tensor(
                    crow2[:, 0:512], crow2[:, 0:512], ublk, op=Op.mult),
                    deps=[tn])
                tn = S.emit("V", lambda e: e.tensor_scalar(
                    crow2[:, 0:512], crow2[:, 0:512], -0.5, 1.5,
                    op0=Op.mult, op1=Op.add), deps=[tn])
                tr2 = S.emit("V", lambda e, ublk=ublk, fold=fold:
                             e.scalar_tensor_tensor(
                                 ublk, D4e, fold, crow2[:, 0:512],
                                 op0=Op.mult, op1=Op.mult), deps=[tn])
            hs[("rsq", h)] = tr2
            tgk = []
            gexd = [hs.get(("l2e", h)), hs.get(("coef", h - 1))]
            for s in range(4):
                gps = PB[6 + (s % 2)]
                tgm = S.emit("T", lambda e, gps=gps, s=s, h=h: MM(
                    e, gps[0:1, 0:512], eye16[0:16, h:h + 1],
                    g64[0:16, 512 * s:512 * s + 512], True, True),
                    deps=[tok_g64, te1, te2] + gexd +
                    ([tgk[s - 2]] if s >= 2 else []))
                tgk.append(S.emit("A", lambda e, gps=gps, s=s: e.activation(
                    G4[32 * s:32 * s + 1, 0:512], gps[0:1, 0:512], AF.Copy),
                    deps=[tgm]))
            hs[("gex", h)] = S.tok("a")
            tdm = None
            d4dep = [hs.get(("d4e", h - 1))]
            for s in range(4):
                tp = S.emit("V", lambda e, s=s: e.tensor_tensor(
                    prodb[:, 0:512], qc[:, HLO:TH], kc_[:, HLO - s:TH - s],
                    op=Op.mult), deps=[tsk, tq, tdm] +
                    ([hs.get(("vacc", h - 1, 3))] if s == 0 else []))
                tdm = S.emit("T", lambda e, s=s: MM(
                    e, PB[4][:, 0:512], ohB[:, 128 * s:128 * s + 128],
                    prodb[:, 0:512], s == 0, s == 3),
                    deps=[tp, tok_g0] + d4dep)
            tce = S.emit("A", lambda e: e.activation(
                D4e, PB[4][:, 0:512], AF.Copy),
                deps=[tdm, hs.get(("coef", h - 1)),
                      hs.get(("lastst", h - 1)), hs[("rsq", h)]])
            hs[("d4e", h)] = tce
            hs[("d4f", h)] = tce
            tcf = S.emit("V", lambda e: e.tensor_tensor(
                D4e, D4e, RQ4, op=Op.mult), deps=[tce, tr2])
            tcf = S.emit("V", lambda e: e.tensor_tensor(
                D4e, D4e, RK4, op=Op.mult), deps=[tcf])
            tcf = S.emit("V", lambda e: e.tensor_tensor(
                D4e, D4e, G4, op=Op.mult), deps=[tcf] + tgk)
            hs[("coef", h)] = tcf
            outm = pf(O_OUT + h * 512, 512)
            for s in range(4):
                cb = PB[2 + (s % 2)]
                tst = S.emit("V", lambda e, s=s: e.tensor_copy(
                    crow2[0:1, 0:512], D4e[32 * s:32 * s + 1, 0:512]),
                    deps=[tcf] + ([hs.get(("bc", h, s - 1))] if s > 0 else
                                  [hs.get(("bc", h - 1, 3))]))
                tbc = S.emit("T", lambda e, cb=cb: MM(
                    e, cb[:, 0:512], ones_r[:], crow2[0:1, 0:512], True,
                    True), deps=[tst, hs.get(("vacc", h, s - 2)) if s >= 2
                                 else hs.get(("vacc", h - 1, s + 2))])
                hs[("bc", h, s)] = tbc
                if s == 0:
                    tva = S.emit("V", lambda e, cb=cb, outm=outm:
                                 e.tensor_tensor(outm, cb[:, 0:512],
                                                 vpost[:, HLO:TH],
                                                 op=Op.mult),
                                 deps=[tbc, tv_])
                else:
                    tva = S.emit("V", lambda e, cb=cb, s=s: e.tensor_tensor(
                        prodb[:, 0:512], cb[:, 0:512],
                        vpost[:, HLO - s:TH - s],
                        op=Op.mult), deps=[tbc, tv_])
                    tva = S.emit("V", lambda e, outm=outm: e.tensor_tensor(
                        outm, outm, prodb[:, 0:512], op=Op.add), deps=[tva])
                hs[("vacc", h, s)] = tva
                hs[("lastst", h)] = tst
            if tile == 0 and h == 0:
                dump("qc0", qc, hs[("coef", 0)], [128, 520])
                dump("kc0", kc_, hs[("coef", 0)], [128, 520])
                dump("rqrk0", pf(O_ROW + 512, 1024), hs[("rsq", 0)],
                     [128, 1024])
                dump("g40", G4, hs[("gex", 0)], [128, 512])
                dump("d4e0", D4e, hs[("coef", 0)], [128, 512])
                dump("out0", pf(O_OUT, 512), hs[("vacc", 0, 3)], [128, 512])

        gb = (N_GAM // H + 1) if tile == 0 else 0
        for h in range(H):
            front_mm(h, 0)
            front_mm(h, 1)
            gamma_step(gb)
            if h > 0:
                head_back(h - 1)
            front_evac(h, 0)
            front_evac(h, 1)
            front_mm(h, 2)
            front_evac(h, 2)
        head_back(H - 1)
        tok_mix_v = S.tok("v")
        tok_mix_a = S.tok("a")
        tok_mix_t = S.tok("t")

        if tile == 0:
            gamma_step(N_GAM)
            tga = S.tok("a")
            rs = []
            for wi, (a0, b0) in enumerate(((0, 88), (88, 176), (176, N_GAM))):
                rs.append(S.emit("V", lambda e, wi=wi, a0=a0, b0=b0:
                                 e.reduce_sum(pf(GACC + 280, 8)[:, wi:wi + 1],
                                              pf(GACC + a0, b0 - a0),
                                              axis=AX.X), deps=[tga]))
            tsum = S.emit("T", lambda e: MM(
                e, PB[7][0:1, 0:3], ones_c[:],
                pf(GACC + 280, 8)[:, 0:3], True, True),
                deps=rs + [tok_mix_a, tok_mix_v])
            tth = S.emit("A", lambda e: e.activation(
                grow[0:1, 0:3], PB[7][0:1, 0:3], AF.Copy, scale=0.5 / NW,
                bias=0.5e-8), deps=[tsum])
            tgc = S.emit("T", lambda e: MM(
                e, PB[2][:, 0:3], ones_r[:], grow[0:1, 0:3], True, True),
                deps=[tth, tok_mix_v])
            tgc = S.emit("A", lambda e: e.activation(
                gam_c[:, 0:3], PB[2][:, 0:3], AF.Copy), deps=[tgc])
            tok_gam = S.emit("V", lambda e: e.tensor_scalar(
                negc[:, 0:3], gam_c[:, 0:3], -1.0, None, op0=Op.mult),
                deps=[tgc])
            build_nc.tok_gam = tok_gam
        else:
            tok_gam = build_nc.tok_gam

        sqb = [pf(O_SQ, 520), pf(O_AB + 520, 520)]
        lastq = [None, None]
        for ch in range(KC):
            sq = sqb[ch % 2]
            tsq = S.emit("V", lambda e, ch=ch, sq=sq: e.tensor_tensor(
                sq[:, 0:T], pf(O_OUT + ch * 512, 512),
                pf(O_OUT + ch * 512, 512), op=Op.mult),
                deps=([tok_mix_v] if ch == 0 else []) + [lastq[ch % 2]])
            lastq[ch % 2] = S.emit("T", lambda e, ch=ch, sq=sq: MM(
                e, PB[4][0:1, 0:512], ones_c[:], sq[:, 0:T], ch == 0,
                ch == KC - 1), deps=[tsq, tok_mix_a] if ch == 0 else [tsq])
        rmu = pf(O_LR, 520)[0:1, 0:T]
        rmsc = pf(O_LR + 520, 520)[0:1, 0:T]
        trow = S.emit("V", lambda e: e.tensor_scalar(
            rmu, PB[4][0:1, 0:512], 1.0 / DM, 1.0e-6, op0=Op.mult,
            op1=Op.add), deps=[S.tok("t")])
        tq0 = S.emit("A", lambda e: e.activation(rmsc, rmu, AF.Ln),
                     deps=[trow])
        tq0 = S.emit("A", lambda e: e.activation(rmsc, rmsc, AF.Exp,
                                                 scale=-0.5), deps=[tq0])
        tn = S.emit("V", lambda e: e.tensor_tensor(
            pf(O_LR + 1040, 520)[0:1, 0:T], rmsc, rmsc, op=Op.mult),
            deps=[tq0])
        tn = S.emit("V", lambda e: e.tensor_tensor(
            pf(O_LR + 1040, 520)[0:1, 0:T], pf(O_LR + 1040, 520)[0:1, 0:T],
            rmu, op=Op.mult), deps=[tn])
        tn = S.emit("V", lambda e: e.tensor_scalar(
            pf(O_LR + 1040, 520)[0:1, 0:T], pf(O_LR + 1040, 520)[0:1, 0:T],
            -0.5, 1.5, op0=Op.mult, op1=Op.add), deps=[tn])
        trow = S.emit("V", lambda e: e.tensor_tensor(
            rmu, rmsc, pf(O_LR + 1040, 520)[0:1, 0:T], op=Op.mult),
            deps=[tn])
        trb = S.emit("T", lambda e: MM(
            e, PB[2][:, 0:512], ones_r[:], pf(O_LR, 520)[0:1, 0:T], True,
            True), deps=[trow, tok_mix_v])
        if tile == 0:
            dump("outF", pf(O_OUT, 8192), tok_mix_v, [128, 8192])
            dump("rms", pf(O_LR, 520), trow, [128, 520])
            dump("gamc", gam_c[:], tok_gam, [128, 3])
        zsil = pf(O_ZS, 512)
        gtmp = pf(O_ZS + 512, 512)
        wstate["split"] = tok_mix_t
        wstate["mmfree"] = [tok_mix_t, tok_mix_t]
        tok_g_done = []
        for m in range(KC):
            whi, wlo, wdeps = load_w_slab(d_wz, m, m % 2)
            zp = PB[m % 2]
            tl2 = None
            for ch in range(KC):
                whic = whi[:, ch * 128:(ch + 1) * 128]
                wloc = wlo[:, ch * 128:(ch + 1) * 128]
                tl2 = mm3(zp[:, 0:512], whic, wloc, ch, HLO, 512, ch == 0,
                          ch == KC - 1,
                          deps=(wdeps + [tok_mix_v,
                                         tok_g_done[m - 2] if m >= 2 else
                                         None]) if ch == 0 else [])
            wstate["mmfree"][m % 2] = tl2
            tz = S.emit("A", lambda e, zp=zp: e.activation(
                zsil, zp[:, 0:512], AF.Silu),
                deps=[tl2, tok_g_done[m - 1] if m >= 1 else None])
            tg1 = S.emit("V", lambda e, m=m: e.tensor_tensor(
                gtmp, pf(O_OUT + m * 512, 512), PB[2][:, 0:512], op=Op.mult),
                deps=[trb, tz])
            tg1 = S.emit("V", lambda e: e.tensor_tensor(
                gtmp, gtmp, zsil, op=Op.mult), deps=[tg1])
            tg1 = S.emit("V", lambda e, m=m: e.tensor_scalar(
                gtmp, gtmp, nwt[:, m:m + 1], None, op0=Op.mult),
                deps=[tg1])
            ghi = pbf(O_OUT + m * 512, 512)
            glo = pool_bf[:, 2 * (O_OUT + m * 512) + 512:
                          2 * (O_OUT + m * 512) + 1024]
            th2 = S.emit("A", lambda e, ghi=ghi: e.activation(
                ghi, gtmp, AF.Copy), deps=[tg1])
            tok_g_done.append(S.emit("V", lambda e, glo=glo, ghi=ghi:
                                     e.tensor_tensor(glo, gtmp, ghi,
                                                     op=Op.subtract),
                                     deps=[th2]))

        def g_hi(ch):
            return pbf(O_OUT + ch * 512, 512)

        def g_lo(ch):
            return pool_bf[:, 2 * (O_OUT + ch * 512) + 512:
                           2 * (O_OUT + ch * 512) + 1024]

        tok_ao = []
        for m in range(KC):
            whi, wlo, wdeps = load_w_slab(d_wo, m, m % 2)
            ap_ = PB[m % 2]
            tl2 = None
            for ch in range(KC):
                whic = whi[:, ch * 128:(ch + 1) * 128]
                wloc = wlo[:, ch * 128:(ch + 1) * 128]
                deps0 = (wdeps + [tok_g_done[-1],
                                  tok_ao[m - 2] if m >= 2 else None]) \
                    if ch == 0 else []
                S.emit("T", lambda e, ap_=ap_, whic=whic, ch=ch: MM(
                    e, ap_[:, 0:512], whic, g_hi(ch), ch == 0, False),
                    deps=deps0)
                S.emit("T", lambda e, ap_=ap_, whic=whic, ch=ch: MM(
                    e, ap_[:, 0:512], whic, g_lo(ch), False, False))
                tl2 = S.emit("T", lambda e, ap_=ap_, wloc=wloc, ch=ch: MM(
                    e, ap_[:, 0:512], wloc, g_hi(ch), False, ch == KC - 1))
            wstate["mmfree"][m % 2] = tl2
            tok_ao.append(S.emit("A", lambda e, m=m, ap_=ap_: e.activation(
                ao_ch(m), ap_[:, 0:512], AF.Copy), deps=[tl2]))

        if tile == 0:
            dump("ao0", ao_ch(0), tok_ao[0], [128, T])
        tok_res1 = resid_ln(False, [tok_ao[-1]])
        if tile == 0:
            dump("xr1", xt_ch(0), tok_res1, [128, TH])

        tok_h2_v, tok_h2_a = layernorm_to(
            False, [tok_res1], lambda ch: h_hi(ch, HLO, 512),
            lambda ch: h_lo(ch, HLO, 512))

        sigb = pf(F_SC, 512)
        sq2 = pf(F_SC + 512, 512)
        tmpb = pbf(F_SC, 2048)
        fst = {"d1": None, "d2": None, "q1mm": [None, None],
               "q2mm": [None, None], "tmpfree": None, "hidfree": None}

        def hid_hi(k):
            return pool_bf[:, 2 * (F_HID + k * 512): 2 * (F_HID + k * 512) + 512]

        def hid_lo(k):
            return pool_bf[:, 2 * (F_HID + k * 512) + 512:
                           2 * (F_HID + k * 512) + 1024]

        tok_hid = []
        tok_sg = []
        for m in range(MC):
            par = m % 2
            stg1 = pf(F_ST, 2048)
            stg2 = pf(F_ST + 2048, 2048)
            q1 = pbf(F_Q + par * 2048, 2048)
            q2 = pbf(F_Q + par * 2048 + 1024, 2048)
            d1 = S.emit("S", lambda e, m=m, stg1=stg1: e.dma_start(
                out=stg1, in_=d_w1[m * 128:(m + 1) * 128, :]),
                deps=[fst["d1"]] + ([tok_ao[-1], tok_h2_v, tok_h2_a]
                                    if m == 0 else []), dma="sd")
            d2 = S.emit("S", lambda e, m=m, stg2=stg2: e.dma_start(
                out=stg2, in_=d_w2[m * 128:(m + 1) * 128, :]),
                deps=[fst["d2"]] + ([tok_h2_v] if m == 0 else []), dma="sd")
            ts1 = S.emit("A", lambda e, q1=q1, stg1=stg1: e.activation(
                q1, stg1, AF.Sign, bias=negc[:, 0:1]),
                deps=[d1, tok_gam, fst["q1mm"][par], tok_h2_v])
            ts2 = S.emit("A", lambda e, stg1=stg1: e.activation(
                tmpb, stg1, AF.Sign, bias=gam_c[:, 0:1]),
                deps=[ts1, fst["tmpfree"]])
            tq1 = S.emit("V", lambda e, q1=q1: e.tensor_tensor(
                q1, q1, tmpb, op=Op.add), deps=[ts2])
            ts1b = S.emit("A", lambda e, q2=q2, stg2=stg2: e.activation(
                q2, stg2, AF.Sign, bias=negc[:, 1:2]),
                deps=[d2, tok_gam, fst["q2mm"][par]])
            ts2b = S.emit("A", lambda e, stg2=stg2: e.activation(
                tmpb, stg2, AF.Sign, bias=gam_c[:, 1:2]),
                deps=[ts1b, tq1])
            tq2 = S.emit("V", lambda e, q2=q2: e.tensor_tensor(
                q2, q2, tmpb, op=Op.add), deps=[ts2b])
            fst["d1"] = tq1
            fst["d2"] = tq2
            x1p = PB[par]
            x2p = PB[2 + par]
            tlast1 = None
            tlast2 = None
            for ch in range(KC):
                q1c = q1[:, ch * 128:(ch + 1) * 128]
                deps0 = [tq1, tok_h2_v, tok_h2_a,
                         tok_sg[m - 2] if m >= 2 else None] if ch == 0 else []
                S.emit("T", lambda e, x1p=x1p, q1c=q1c, ch=ch: MM(
                    e, x1p[:, 0:512], q1c, h_hi(ch, HLO, 512), ch == 0,
                    False), deps=deps0)
                tlast1 = S.emit("T", lambda e, x1p=x1p, q1c=q1c, ch=ch: MM(
                    e, x1p[:, 0:512], q1c, h_lo(ch, HLO, 512), False,
                    ch == KC - 1))
            for ch in range(KC):
                q2c = q2[:, ch * 128:(ch + 1) * 128]
                deps0 = [tq2, tok_hid[m - 2] if m >= 2 else None] \
                    if ch == 0 else []
                S.emit("T", lambda e, x2p=x2p, q2c=q2c, ch=ch: MM(
                    e, x2p[:, 0:512], q2c, h_hi(ch, HLO, 512), ch == 0,
                    False), deps=deps0)
                tlast2 = S.emit("T", lambda e, x2p=x2p, q2c=q2c, ch=ch: MM(
                    e, x2p[:, 0:512], q2c, h_lo(ch, HLO, 512), False,
                    ch == KC - 1))
            fst["q1mm"][par] = tlast1
            fst["q2mm"][par] = tlast2
            tsg = S.emit("A", lambda e, x1p=x1p: e.activation(
                sigb, x1p[:, 0:512], AF.Sigmoid, scale=0.5),
                deps=[tlast1, tq2, fst["hidfree"]])
            tok_sg.append(tsg)
            thd = S.emit("V", lambda e, x2p=x2p: e.scalar_tensor_tensor(
                sq2, x2p[:, 0:512], 0.5, sigb, op0=Op.mult, op1=Op.mult),
                deps=[tsg, tlast2])
            thh = S.emit("A", lambda e, m=m: e.activation(
                hid_hi(m), sq2, AF.Copy), deps=[thd])
            tlo = S.emit("V", lambda e, m=m: e.tensor_tensor(
                hid_lo(m), sq2, hid_hi(m), op=Op.subtract), deps=[thh])
            tok_hid.append(tlo)
            fst["tmpfree"] = tsg
            fst["hidfree"] = tlo
        tok_hid_all = S.tok("v")

        w3s = {"d": [None, None], "q": [None, None]}
        tok_ao2 = []
        for m in range(KC):
            ap2 = PB[4 + (m % 2)] if False else PB[m % 2]
            tl2 = None
            for half in range(4):
                p2 = half % 2
                k0 = half * 11
                stg = pf(W3_ST + p2 * 1408, 1408)
                q3 = pbf(W3_Q + p2 * 704, 1408)
                d3 = S.emit("S", lambda e, m=m, k0=k0, stg=stg: e.dma_start(
                    out=stg, in_=d_w3[m * 128:(m + 1) * 128,
                                      k0 * 128:(k0 + 11) * 128]),
                    deps=[w3s["d"][p2]] + ([tok_hid_all, tok_sg[-1]]
                                           if m == 0 and half == 0 else []),
                    dma="sd")
                tsa = S.emit("A", lambda e, q3=q3, stg=stg: e.activation(
                    q3, stg, AF.Sign, bias=negc[:, 2:3]),
                    deps=[d3, tok_gam, w3s["q"][p2]])
                tsb = S.emit("A", lambda e, stg=stg: e.activation(
                    tmpb[:, 0:1408], stg, AF.Sign, bias=gam_c[:, 2:3]),
                    deps=[tsa, tok_sg[-1] if m == 0 and half == 0 else None])
                tq3 = S.emit("V", lambda e, q3=q3: e.tensor_tensor(
                    q3, q3, tmpb[:, 0:1408], op=Op.add), deps=[tsb])
                w3s["d"][p2] = tq3
                for kk in range(11):
                    k = k0 + kk
                    q3c = q3[:, kk * 128:(kk + 1) * 128]
                    deps0 = [tq3] + ([tok_ao2[m - 2]] if m >= 2 and k == 0
                                     else [])
                    S.emit("T", lambda e, ap2=ap2, q3c=q3c, k=k: MM(
                        e, ap2[:, 0:512], q3c, hid_hi(k), k == 0, False),
                        deps=deps0)
                    tl2 = S.emit("T", lambda e, ap2=ap2, q3c=q3c, k=k: MM(
                        e, ap2[:, 0:512], q3c, hid_lo(k), False,
                        k == MC - 1))
                w3s["q"][p2] = tl2
            tok_ao2.append(S.emit("A", lambda e, m=m, ap2=ap2: e.activation(
                ao_ch(m), ap2[:, 0:512], AF.Copy), deps=[tl2]))

        if tile == 0:
            dump("hid0", pool_bf[:, 0:1024], tok_hid[0], [128, 1024], BF16)
            dump("ao20", ao_ch(0), tok_ao2[0], [128, T])
        tok_res2 = resid_ln(True, [tok_ao2[-1]])
        if tile == 0:
            dump("xf0", xt_ch(0), tok_res2, [128, TH])

        for ch in range(KC):
            src = xt_ch(ch, HLO, T)
            dst = d_out[ch * 128:(ch + 1) * 128, tile * T:(tile + 1) * T]
            S.emit("G", lambda e, src=src, dst=dst: e.dma_start(
                out=dst, in_=src), deps=[tok_res2], dma="gd")
        barrier = [S.tok("t"), S.tok("v"), S.tok("a"), S.tok("sd"),
                   S.tok("gd")]
        S.flush(blk)

    S.emit("G", lambda e: e.memset(fin[:], 1.0), deps=[S.tok("gd")])
    S.flush(blk)

    es.close()
    return nc


def _stage_kxm(w, kc, mc):
    return np.ascontiguousarray(
        np.asarray(w, np.float32).reshape(kc, 128, mc, 128)
        .transpose(2, 1, 0, 3).reshape(mc * 128, kc * 128))


def kernel(L0_attn_x, L0_attn_Wq, L0_attn_Wk, L0_attn_Wv, L0_attn_Wz,
           L0_attn_Wa, L0_attn_Wb, L0_attn_conv_q_w, L0_attn_conv_k_w,
           L0_attn_conv_v_w, L0_attn_norm_w, L0_attn_Wout,
           L0_ffn_w1, L0_ffn_w2, L0_ffn_w3):
    x = np.ascontiguousarray(np.asarray(L0_attn_x, dtype=np.float32))
    B, Sq, _ = x.shape
    x2d = x.reshape(B * Sq, DM)

    wq = _stage_kxm(L0_attn_Wq, KC, KC)
    wk = _stage_kxm(L0_attn_Wk, KC, KC)
    wv = _stage_kxm(L0_attn_Wv, KC, KC)
    wz = _stage_kxm(L0_attn_Wz, KC, KC)
    wo = _stage_kxm(L0_attn_Wout, KC, KC)
    w1t = _stage_kxm(np.ascontiguousarray(np.asarray(L0_ffn_w1, np.float32).T),
                     KC, MC)
    w2t = _stage_kxm(np.ascontiguousarray(np.asarray(L0_ffn_w2, np.float32).T),
                     KC, MC)
    w3t = _stage_kxm(np.ascontiguousarray(np.asarray(L0_ffn_w3, np.float32).T),
                     MC, KC)
    wa = np.asarray(L0_attn_Wa, np.float32).reshape(KC, 128, H).transpose(
        1, 0, 2).reshape(128, KC * H)
    wb = np.asarray(L0_attn_Wb, np.float32).reshape(KC, 128, H).transpose(
        1, 0, 2).reshape(128, KC * H)
    wab = np.ascontiguousarray(np.concatenate([wa, wb], axis=1))
    cw = np.concatenate(
        [np.asarray(L0_attn_conv_q_w, np.float32).reshape(H, 128, 4),
         np.asarray(L0_attn_conv_k_w, np.float32).reshape(H, 128, 4),
         np.asarray(L0_attn_conv_v_w, np.float32).reshape(H, 128, 4)],
        axis=2)
    cw = np.ascontiguousarray(cw.transpose(1, 0, 2).reshape(128, H * 12))
    nw = np.ascontiguousarray(
        np.asarray(L0_attn_norm_w, np.float32).reshape(KC, 128).T)

    nc = build_nc()

    in_maps = []
    for c in range(8):
        b = c // 4
        s0 = (c % 4) * TOK
        xs = np.zeros((HLO + TOK, DM), np.float32)
        r0 = b * Sq + s0
        xs[HLO:] = x2d[r0:r0 + TOK]
        if s0 > 0:
            xs[:HLO] = x2d[r0 - HLO:r0]
        in_maps.append({
            "xt": np.ascontiguousarray(xs.T), "wq": wq, "wk": wk, "wv": wv,
            "wz": wz, "wo": wo, "wab": wab, "cw": cw, "nw": nw,
            "pos0": np.full((128, 1), float(s0 - HLO), np.float32),
            "w1t": w1t, "w2t": w2t, "w3t": w3t,
        })

    res = run_bass_kernel_spmd(nc, in_maps, core_ids=list(range(8)))

    out = np.empty((B, Sq, DM), np.float32)
    for c in range(8):
        b = c // 4
        s0 = (c % 4) * TOK
        out[b, s0:s0 + TOK] = res.results[c]["out"].T
    return out


# version 06 (final)
# speedup vs baseline: 1.0271x; 1.0052x over previous
import math
import contextlib
import numpy as np

import concourse.bass as bass
import concourse.mybir as mybir
from concourse.bass_utils import run_bass_kernel_spmd
from concourse.alu_op_type import AluOpType as Op

F32 = mybir.dt.float32
BF16 = mybir.dt.bfloat16
AF = mybir.ActivationFunctionType
AX = mybir.AxisListType

DM = 2048
H = 16
KC = 16
FH = 5632
MC = 44
T = 512
HLO = 6
TH = T + HLO
NTILE = 2
TOK = NTILE * T
LN1E6 = math.log(1.0e6)
TWO_PI = 2.0 * math.pi
NW = float(FH * DM)


class Sched:
    ENGS = ("T", "V", "A", "G", "S")

    def __init__(self, sems):
        self.sems = sems
        self.cnt = {k: 0 for k in sems}
        self.seen = {e: {k: 0 for k in sems} for e in self.ENGS}
        self.plan = {e: [] for e in self.ENGS}
        self.own = {"T": "t", "V": "v", "A": "a", "G": "g", "S": "s"}

    def emit(self, eng, fn, deps=(), dma=None, sync_self=False):
        sem = dma if dma else self.own[eng]
        step = 16 if dma else 1
        self.plan[eng].append(([d for d in deps if d is not None], fn, sem,
                               step, sync_self))
        self.cnt[sem] += step
        return (sem, self.cnt[sem])

    def tok(self, sem):
        return (sem, self.cnt[sem])

    def flush(self, block):
        decs = {"T": block.tensor, "V": block.vector, "A": block.scalar,
                "G": block.gpsimd, "S": block.sync}
        for eng in self.ENGS:
            ops = self.plan[eng]
            if not ops:
                continue
            self.plan[eng] = []
            seen = self.seen[eng]
            sems = self.sems
            own = self.own[eng]

            def body(e, ops=ops, seen=seen, own=own, sems=sems):
                for deps, fn, sem, step, sself in ops:
                    for ds, dv in deps:
                        if (ds != own or sself) and seen[ds] < dv:
                            e.wait_ge(sems[ds], dv)
                            seen[ds] = dv
                    fn(e).then_inc(sems[sem], step)
            decs[eng](body)


def build_nc(dbg=False):
    nc = bass.Bass("TRN2", target_bir_lowering=False)
    dbg_specs = []

    def mkdump(name, shape, dt=F32):
        ap = nc.dram_tensor("dbg_" + name, shape, dt,
                            kind="ExternalOutput").ap()
        dbg_specs.append(name)
        return ap

    d_xt = nc.dram_tensor("xt", [DM, HLO + TOK], F32, kind="ExternalInput").ap()
    d_wq = nc.dram_tensor("wq", [DM, DM], F32, kind="ExternalInput").ap()
    d_wk = nc.dram_tensor("wk", [DM, DM], F32, kind="ExternalInput").ap()
    d_wv = nc.dram_tensor("wv", [DM, DM], F32, kind="ExternalInput").ap()
    d_wz = nc.dram_tensor("wz", [DM, DM], F32, kind="ExternalInput").ap()
    d_wo = nc.dram_tensor("wo", [DM, DM], F32, kind="ExternalInput").ap()
    d_wab = nc.dram_tensor("wab", [128, 2 * KC * H], F32, kind="ExternalInput").ap()
    d_cw = nc.dram_tensor("cw", [128, H * 12], F32, kind="ExternalInput").ap()
    d_nw = nc.dram_tensor("nw", [128, KC], F32, kind="ExternalInput").ap()
    d_pos0 = nc.dram_tensor("pos0", [128, 1], F32, kind="ExternalInput").ap()
    d_w1 = nc.dram_tensor("w1t", [FH, DM], F32, kind="ExternalInput").ap()
    d_w2 = nc.dram_tensor("w2t", [FH, DM], F32, kind="ExternalInput").ap()
    d_w3 = nc.dram_tensor("w3t", [DM, FH], F32, kind="ExternalInput").ap()
    d_out = nc.dram_tensor("out", [DM, TOK], F32, kind="ExternalOutput").ap()

    es = contextlib.ExitStack()

    def sb(name, shape, dt=F32):
        return es.enter_context(nc.sbuf_tensor(name, shape, dt))

    xt = sb("xts", [128, KC * TH])
    slot = sb("slot", [128, KC * TH])
    slot_bf = slot[:].bitcast(BF16)

    POOLW = 31770
    pool = sb("pool", [128, POOLW])
    pool_bf = pool[:].bitcast(BF16)

    ones_c = sb("ones_c", [128, 1])
    ones_r = sb("ones_r", [1, 128])
    half_c = sb("half_c", [128, 1])
    quar_c = sb("quar_c", [128, 1])
    iot_p = sb("iot_p", [128, 1])
    inv_c = sb("inv_c", [128, 1])
    pos0 = sb("pos0s", [128, 1])
    nwt = sb("nwt", [128, KC])
    cwt = sb("cwt", [128, H * 12])
    wab = sb("wabs", [128, 2 * KC * H])
    wab_h = sb("wab_h", [128, 2 * KC * H], BF16)
    wab_l = sb("wab_l", [128, 2 * KC * H], BF16)
    gam_c = sb("gam_c", [128, 3])
    negc = sb("negc", [128, 3])
    grow = sb("grow", [1, 8])
    ohB = sb("ohB", [128, 512])
    eye16 = sb("eye16", [16, 16])
    i16 = sb("i16", [16, 16])
    epsc = sb("epsc", [128, 3])
    grow2 = sb("grow2", [128, 3])
    fin = sb("fin", [1, 1])

    O_OUT = 0
    O_QKV = 8192
    O_WST = O_QKV + 3640
    O_WBF = O_WST + 2048
    O_GAM = O_WBF + 4096
    GACC = O_GAM + 2048
    O_G64 = GACC + 288
    O_ROW = O_G64 + 2048
    O_CROW = O_ROW + 2560
    O_SQ = O_CROW + 520
    O_CS = O_SQ + 520
    O_AB = O_CS + 1040
    O_LR = O_AB + 1040
    ATTN_END = O_LR + 2080
    assert ATTN_END <= POOLW, ATTN_END
    O_ZS = O_ROW + 1536
    O_GT = O_ROW + 2048

    F_HID = 0
    F_ST = 22528
    F_Q = F_ST + 4096
    F_SC = F_Q + 4096
    FFN_END = F_SC + 1024
    assert FFN_END <= POOLW, FFN_END
    W3_ST = F_ST
    W3_Q = F_ST + 2816

    def pf(off, n):
        return pool[:, off:off + n]

    def pbf(off_f32, n_bf16):
        return pool_bf[:, 2 * off_f32: 2 * off_f32 + n_bf16]

    sems = {}
    for name in ("t", "v", "a", "g", "s", "sd", "gd"):
        sems[name] = es.enter_context(nc.semaphore(name))
    S = Sched(sems)
    S.extra = None

    _orig_emit = S.emit

    def _emit2(eng, fn, deps=(), dma=None, sync_self=False):
        dl = list(deps)
        if S.extra is not None:
            dl.append(S.extra)
        return _orig_emit(eng, fn, deps=dl, dma=dma, sync_self=sync_self)
    if dbg:
        S.emit = _emit2

    def dump(name, src, dep, shape, dt=F32):
        if not dbg:
            return
        d = mkdump(name, shape, dt)
        tok = _orig_emit("G", lambda e, d=d, src=src: e.dma_start(
            out=d, in_=src), deps=[x for x in [dep, S.extra] if x], dma="gd")
        S.extra = tok

    PB = [es.enter_context(nc.psum_tensor("pb%d" % i, [128, 512], F32))
          for i in range(8)]

    def MM(e, out, lhsT, rhs, start, stop):
        return e.matmul(out, lhsT, rhs, start=start, stop=stop,
                        skip_group_check=True)

    blk = es.enter_context(nc.Block())

    def h_hi(ch, c0=0, n=TH):
        b = 2 * ch * TH
        return slot_bf[:, b + c0: b + c0 + n]

    def h_lo(ch, c0=0, n=TH):
        b = 2 * ch * TH + TH
        return slot_bf[:, b + c0: b + c0 + n]

    def ao_ch(ch):
        return slot[:, ch * TH: ch * TH + T]

    def xt_ch(ch, c0=0, n=TH):
        return xt[:, ch * TH + c0: ch * TH + c0 + n]

    S.emit("G", lambda e: e.memset(ones_c[:], 1.0))
    S.emit("G", lambda e: e.memset(ones_r[:], 1.0))
    S.emit("G", lambda e: e.memset(half_c[:], 0.5))
    S.emit("G", lambda e: e.memset(quar_c[:], 0.25))
    tok_g0 = S.emit("G", lambda e: e.iota(
        iot_p[:], pattern=[[0, 1]], base=0, channel_multiplier=1,
        allow_small_or_imprecise_dtypes=True))
    t_ohb = S.emit("G", lambda e: e.memset(ohB[:], 0.0))
    for s4 in range(4):
        t_ohb = S.emit("G", lambda e, s4=s4: e.memset(
            ohB[:, 128 * s4 + 32 * s4: 128 * s4 + 32 * s4 + 1], 1.0),
            deps=[t_ohb], sync_self=True)
    tk16 = S.emit("G", lambda e: e.iota(
        i16[:], pattern=[[1, 16]], base=0, channel_multiplier=0,
        allow_small_or_imprecise_dtypes=True))
    S.emit("V", lambda e: e.tensor_scalar(
        eye16[:], i16[:], iot_p[0:16, 0:1], None, op0=Op.is_equal),
        deps=[tk16, tok_g0])
    S.emit("G", lambda e: e.memset(epsc[:, 0:1], 1.0e-5))
    S.emit("G", lambda e: e.memset(epsc[:, 1:2], 1.0e-6))
    S.emit("G", lambda e: e.memset(epsc[:, 2:3], 0.5e-8))
    S.emit("S", lambda e: e.dma_start(out=pos0[:], in_=d_pos0), dma="sd")
    S.emit("S", lambda e: e.dma_start(out=nwt[:], in_=d_nw), dma="sd")
    S.emit("S", lambda e: e.dma_start(out=cwt[:], in_=d_cw), dma="sd")
    tok_small = S.emit("S", lambda e: e.dma_start(out=wab[:], in_=d_wab),
                       dma="sd")
    th_ = S.emit("A", lambda e: e.activation(wab_h[:], wab[:], AF.Copy),
                 deps=[tok_small])
    tok_wab = S.emit("V", lambda e: e.tensor_tensor(
        wab_l[:], wab[:], wab_h[:], op=Op.subtract), deps=[th_])
    tge = S.emit("V", lambda e: e.tensor_scalar(
        grow[0:1, 0:1] if False else inv_c[:], iot_p[:], 64.0, None,
        op0=Op.is_ge), deps=[tok_g0])
    tiv = S.emit("V", lambda e: e.scalar_tensor_tensor(
        grow2[:, 0:1], inv_c[:], -64.0, iot_p[:], op0=Op.mult, op1=Op.add),
        deps=[tge], sync_self=True)
    tok_inv = S.emit("V", lambda e: e.memset(inv_c[:], 1.0), deps=[tiv],
                     sync_self=True)
    for j in range(5, -1, -1):
        p2 = float(1 << j)
        r2j = float(1.0e6 ** (-(1 << j) / 64.0))
        tb = S.emit("V", lambda e, p2=p2: e.tensor_scalar(
            grow2[:, 1:2], grow2[:, 0:1], p2, None, op0=Op.is_ge),
            deps=[tiv, tok_inv], sync_self=True)
        tiv = S.emit("V", lambda e, p2=p2: e.scalar_tensor_tensor(
            grow2[:, 0:1], grow2[:, 1:2], -p2, grow2[:, 0:1], op0=Op.mult,
            op1=Op.add), deps=[tb], sync_self=True)
        tf = S.emit("V", lambda e, r2j=r2j: e.tensor_scalar(
            grow2[:, 2:3], grow2[:, 1:2], r2j - 1.0, 1.0, op0=Op.mult,
            op1=Op.add), deps=[tiv], sync_self=True)
        tok_inv = S.emit("V", lambda e: e.tensor_tensor(
            inv_c[:], inv_c[:], grow2[:, 2:3], op=Op.mult), deps=[tf],
            sync_self=True)

    gam_jobs = []
    idx = 0
    for dram, rows in ((d_w1, FH), (d_w2, FH), (d_w3, DM)):
        cols = DM if rows == FH else FH
        for rc in range(rows // 128):
            for cs in range(0, cols, 1024):
                w = min(1024, cols - cs)
                gam_jobs.append((dram, rc * 128, cs, w, idx))
                idx += 1
    N_GAM = len(gam_jobs)
    GLIM = {"w1": 88, "w2": 176, "w3": N_GAM}
    assert N_GAM <= 288 - 16, N_GAM
    gst = {"i": 0, "last": [None, None]}

    def gamma_step(nmax):
        for _ in range(nmax):
            i = gst["i"]
            if i >= N_GAM:
                return
            gst["i"] = i + 1
            dram, r0, c0, w, col = gam_jobs[i]
            buf = pf(O_GAM + (i % 2) * 1024, w)
            td = S.emit("G", lambda e, dram=dram, r0=r0, c0=c0, w=w, buf=buf:
                        e.dma_start(out=buf, in_=dram[r0:r0 + 128, c0:c0 + w]),
                        deps=[gst["last"][i % 2]], dma="gd")
            gst["last"][i % 2] = S.emit(
                "A", lambda e, buf=buf, col=col: e.activation(
                    buf, buf, AF.Abs, accum_out=pf(GACC + col, 1)), deps=[td])

    wstate = {"split": None, "mmfree": [None, None]}

    def load_w_slab(dram, m, par, extra=()):
        stg = pf(O_WST, 2048)
        whi = pbf(O_WBF + par * 2048, 2048)
        wlo = pbf(O_WBF + par * 2048 + 1024, 2048)
        td = S.emit("S", lambda e, dram=dram, m=m, stg=stg: e.dma_start(
            out=stg, in_=dram[m * 128:(m + 1) * 128, :]),
            deps=[wstate["split"]] + list(extra), dma="sd")
        tH = S.emit("A", lambda e, whi=whi, stg=stg: e.activation(
            whi, stg, AF.Copy), deps=[td, wstate["mmfree"][par]])
        tL = S.emit("V", lambda e, wlo=wlo, stg=stg, whi=whi: e.tensor_tensor(
            wlo, stg, whi, op=Op.subtract),
            deps=[tH, wstate["mmfree"][par]])
        wstate["split"] = tL
        return whi, wlo, [tH, tL]

    def mm3(out, whi, wlo, ch, c0, n, start, stop, deps=()):
        S.emit("T", lambda e: MM(e, out, whi, h_hi(ch, c0, n), start, False),
               deps=deps)
        S.emit("T", lambda e: MM(e, out, whi, h_lo(ch, c0, n), False, False))
        return S.emit("T", lambda e: MM(e, out, wlo, h_hi(ch, c0, n), False,
                                        stop))

    def layernorm_to(grid_full, dep_first, out_hi, out_lo):
        c0 = 0 if grid_full else HLO
        n = TH if grid_full else T
        sqb = [pf(O_SQ, 520), pf(O_AB + 520, 520)]
        lastq = [None, None]
        for ch in range(KC):
            src = xt_ch(ch, c0, n)
            sq = sqb[ch % 2]
            S.emit("T", lambda e, src=src, ch=ch: MM(
                e, PB[4][0:1, 0:512], ones_c[:], src[:, 0:512],
                ch == 0, ch == KC - 1), deps=dep_first if ch == 0 else [])
            tq = S.emit("V", lambda e, src=src, n=n, sq=sq: e.tensor_tensor(
                sq[:, 0:n], src, src, op=Op.mult),
                deps=(dep_first if ch == 0 else []) + [lastq[ch % 2]])
            S.emit("T", lambda e, ch=ch, sq=sq: MM(
                e, PB[5][0:1, 0:512], ones_c[:], sq[:, 0:512],
                ch == 0, ch == KC - 1), deps=[tq])
            if grid_full:
                S.emit("T", lambda e, src=src, ch=ch: MM(
                    e, PB[6][0:1, 0:6], ones_c[:], src[:, 512:518],
                    ch == 0, ch == KC - 1))
                lastq[ch % 2] = S.emit("T", lambda e, ch=ch, sq=sq: MM(
                    e, PB[7][0:1, 0:6], ones_c[:], sq[:, 512:518],
                    ch == 0, ch == KC - 1), deps=[tq])
            else:
                lastq[ch % 2] = S.tok("t")
        tmm = S.tok("t")
        mu = pf(O_LR, 520)[0:1, 0:n]
        msq = pf(O_LR + 520, 520)[0:1, 0:n]
        var = pf(O_LR + 1040, 520)[0:1, 0:n]
        ist = pf(O_LR + 1560, 520)[0:1, 0:n]
        S.emit("A", lambda e, mu=mu: e.activation(
            mu[:, 0:512], PB[4][0:1, 0:512], AF.Copy, scale=1.0 / DM),
            deps=[tmm])
        ta = S.emit("A", lambda e, msq=msq: e.activation(
            msq[:, 0:512], PB[5][0:1, 0:512], AF.Copy, scale=1.0 / DM),
            deps=[tmm])
        if grid_full:
            tev1 = S.emit("A", lambda e, mu=mu: e.activation(
                mu[:, 512:518], PB[6][0:1, 0:6], AF.Copy, scale=1.0 / DM),
                deps=[tmm])
            ta = S.emit("A", lambda e, msq=msq: e.activation(
                msq[:, 512:518], PB[7][0:1, 0:6], AF.Copy, scale=1.0 / DM),
                deps=[tmm])
        tv = S.emit("V", lambda e, var=var, mu=mu: e.tensor_tensor(
            var, mu, mu, op=Op.mult), deps=[ta])
        tv = S.emit("V", lambda e, var=var, msq=msq: e.tensor_tensor(
            var, msq, var, op=Op.subtract), deps=[tv])
        tv = S.emit("V", lambda e, var=var: e.tensor_scalar(
            var, var, 1.0, 1.0e-5, op0=Op.mult, op1=Op.add), deps=[tv])
        tis = S.emit("A", lambda e, ist=ist, var=var: e.activation(
            ist, var, AF.Ln), deps=[tv])
        tis = S.emit("A", lambda e, ist=ist: e.activation(
            ist, ist, AF.Exp, scale=-0.5), deps=[tis])
        tis = S.emit("V", lambda e, ist=ist, msq=msq: e.tensor_tensor(
            msq, ist, ist, op=Op.mult), deps=[tis])
        tis = S.emit("V", lambda e, msq=msq, var=var: e.tensor_tensor(
            msq, msq, var, op=Op.mult), deps=[tis])
        tis = S.emit("V", lambda e, msq=msq: e.tensor_scalar(
            msq, msq, -0.5, 1.5, op0=Op.mult, op1=Op.add), deps=[tis])
        tis = S.emit("V", lambda e, ist=ist, msq=msq: e.tensor_tensor(
            ist, ist, msq, op=Op.mult), deps=[tis])
        S.emit("T", lambda e, mu=mu: MM(
            e, PB[2][:, 0:512], ones_r[:], mu[:, 0:512], True, True),
            deps=[tis])
        S.emit("T", lambda e, ist=ist: MM(
            e, PB[3][:, 0:512], ones_r[:], ist[:, 0:512], True, True),
            deps=[tis])
        if grid_full:
            S.emit("T", lambda e, mu=mu: MM(
                e, PB[6][:, 0:6], ones_r[:], mu[:, 512:518], True, True),
                deps=[tis])
            S.emit("T", lambda e, ist=ist: MM(
                e, PB[7][:, 0:6], ones_r[:], ist[:, 512:518], True, True),
                deps=[tis])
        tbc = S.tok("t")
        scr = pf(O_SQ, n)
        for ch in range(KC):
            src = xt_ch(ch, c0, n)
            tm = S.emit("V", lambda e, src=src: e.tensor_tensor(
                scr[:, 0:512], src[:, 0:512], PB[2][:, 0:512],
                op=Op.subtract), deps=[tbc])
            tm = S.emit("V", lambda e: e.tensor_tensor(
                scr[:, 0:512], scr[:, 0:512], PB[3][:, 0:512], op=Op.mult),
                deps=[tm])
            if grid_full:
                tm2 = S.emit("V", lambda e, src=src: e.tensor_tensor(
                    scr[:, 512:518], src[:, 512:518], PB[6][:, 0:6],
                    op=Op.subtract), deps=[tbc])
                tm = S.emit("V", lambda e: e.tensor_tensor(
                    scr[:, 512:518], scr[:, 512:518], PB[7][:, 0:6],
                    op=Op.mult), deps=[tm2, tm], sync_self=True)
            thc = S.emit("A", lambda e, ch=ch: e.activation(
                out_hi(ch), scr, AF.Copy), deps=[tm])
            S.emit("V", lambda e, ch=ch: e.tensor_tensor(
                out_lo(ch), scr, out_hi(ch), op=Op.subtract), deps=[thc])
        return S.tok("v"), S.tok("a")

    def resid_ln(halfed, dep_first):
        oc = half_c if halfed else ones_c
        oq = quar_c if halfed else ones_c
        sqb = [pf(O_SQ, 520), pf(O_AB + 520, 520)]
        lastq = [None, None]
        for ch in range(KC):
            src = ao_ch(ch)
            sq = sqb[ch % 2]
            S.emit("T", lambda e, src=src, ch=ch, oc=oc: MM(
                e, PB[4][0:1, 0:512], oc[:], src, ch == 0, ch == KC - 1),
                deps=dep_first if ch == 0 else [])
            tq = S.emit("V", lambda e, src=src, sq=sq: e.tensor_tensor(
                sq[:, 0:T], src, src, op=Op.mult),
                deps=(dep_first if ch == 0 else []) + [lastq[ch % 2]])
            lastq[ch % 2] = S.emit("T", lambda e, ch=ch, oq=oq, sq=sq: MM(
                e, PB[5][0:1, 0:512], oq[:], sq[:, 0:T], ch == 0,
                ch == KC - 1), deps=[tq])
        tmm = S.tok("t")
        mu = pf(O_LR, 520)[0:1, 0:T]
        msq = pf(O_LR + 520, 520)[0:1, 0:T]
        var = pf(O_LR + 1040, 520)[0:1, 0:T]
        ist = pf(O_LR + 1560, 520)[0:1, 0:T]
        S.emit("A", lambda e, mu=mu: e.activation(
            mu, PB[4][0:1, 0:512], AF.Copy, scale=1.0 / DM), deps=[tmm])
        ta = S.emit("A", lambda e, msq=msq: e.activation(
            msq, PB[5][0:1, 0:512], AF.Copy, scale=1.0 / DM), deps=[tmm])
        tv = S.emit("V", lambda e, var=var, mu=mu: e.tensor_tensor(
            var, mu, mu, op=Op.mult), deps=[ta])
        tv = S.emit("V", lambda e, var=var, msq=msq: e.tensor_tensor(
            var, msq, var, op=Op.subtract), deps=[tv])
        tv = S.emit("V", lambda e, var=var: e.tensor_scalar(
            var, var, 1.0, 1.0e-5, op0=Op.mult, op1=Op.add), deps=[tv])
        tis = S.emit("A", lambda e, ist=ist, var=var: e.activation(
            ist, var, AF.Ln), deps=[tv])
        tis = S.emit("A", lambda e, ist=ist: e.activation(
            ist, ist, AF.Exp, scale=-0.5), deps=[tis])
        tis = S.emit("V", lambda e, ist=ist, msq=msq: e.tensor_tensor(
            msq, ist, ist, op=Op.mult), deps=[tis])
        tis = S.emit("V", lambda e, msq=msq, var=var: e.tensor_tensor(
            msq, msq, var, op=Op.mult), deps=[tis])
        tis = S.emit("V", lambda e, msq=msq: e.tensor_scalar(
            msq, msq, -0.5, 1.5, op0=Op.mult, op1=Op.add), deps=[tis])
        tis = S.emit("V", lambda e, ist=ist, msq=msq: e.tensor_tensor(
            ist, ist, msq, op=Op.mult), deps=[tis])
        S.emit("T", lambda e, mu=mu: MM(
            e, PB[2][:, 0:512], ones_r[:], mu, True, True), deps=[tis])
        S.emit("T", lambda e, ist=ist: MM(
            e, PB[3][:, 0:512], ones_r[:], ist, True, True), deps=[tis])
        tbc = S.tok("t")
        lastv = None
        for ch in range(KC):
            src = ao_ch(ch)
            xdst = xt_ch(ch, HLO, T)
            if halfed:
                tm = S.emit("V", lambda e, src=src: e.scalar_tensor_tensor(
                    pf(O_SQ, T), src, 0.5, PB[2][:, 0:512], op0=Op.mult,
                    op1=Op.subtract), deps=[tbc])
            else:
                tm = S.emit("V", lambda e, src=src: e.tensor_tensor(
                    pf(O_SQ, T), src, PB[2][:, 0:512], op=Op.subtract),
                    deps=[tbc])
            tm = S.emit("V", lambda e: e.tensor_tensor(
                pf(O_SQ, T), pf(O_SQ, T), PB[3][:, 0:512], op=Op.mult),
                deps=[tm])
            lastv = S.emit("V", lambda e, xdst=xdst: e.tensor_tensor(
                xdst, xdst, pf(O_SQ, T), op=Op.add), deps=[tm])
        return lastv

    barrier = []
    for tile in range(NTILE):
        for ch in range(KC):
            src = d_xt[ch * 128:(ch + 1) * 128, tile * T: tile * T + TH]
            S.emit("G", lambda e, ch=ch, src=src: e.dma_start(
                out=xt_ch(ch), in_=src),
                deps=barrier if ch == 0 else [], dma="gd")
        tok_xload = S.tok("gd")

        cosb = pf(O_CS, TH)
        sinb = pf(O_CS + 520, TH)
        u = pf(O_AB, TH)
        uscr = pf(O_AB + 520, TH)
        t0 = S.emit("G", lambda e: e.iota(
            u, pattern=[[1, TH]], base=0, channel_multiplier=0,
            allow_small_or_imprecise_dtypes=True), deps=barrier)
        tl_ = S.emit("V", lambda e, tile=tile: e.tensor_scalar(
            u, u, pos0[:, 0:1], float(tile * T), op0=Op.add, op1=Op.add),
            deps=[t0, tok_small] + barrier, sync_self=True)
        tl_ = S.emit("V", lambda e: e.tensor_scalar(
            u, u, inv_c[:, 0:1], None, op0=Op.mult), deps=[tl_, tok_inv],
            sync_self=True)
        tl_ = S.emit("V", lambda e: e.tensor_scalar(
            u, u, 1.0 / TWO_PI, 1024.0, op0=Op.mult, op1=Op.add),
            deps=[tl_], sync_self=True)
        for j in range(10, -1, -1):
            p2 = float(1 << j)
            tl_ = S.emit("V", lambda e, p2=p2: e.tensor_scalar(
                uscr, u, p2, None, op0=Op.is_ge), deps=[tl_])
            tl_ = S.emit("V", lambda e, p2=p2: e.scalar_tensor_tensor(
                u, uscr, -p2, u, op0=Op.mult, op1=Op.add), deps=[tl_])
        tl_ = S.emit("V", lambda e: e.tensor_scalar(
            uscr, u, 0.5, None, op0=Op.is_ge), deps=[tl_])
        tl_ = S.emit("V", lambda e: e.scalar_tensor_tensor(
            u, uscr, -1.0, u, op0=Op.mult, op1=Op.add), deps=[tl_])
        if tile == 0:
            dump("invc", inv_c[:], tok_inv, [128, 1])
            dump("mrow", u, tl_, [128, TH])
        tsn = S.emit("A", lambda e: e.activation(sinb, u, AF.Sin,
                                                 scale=TWO_PI), deps=[tl_])
        tl_ = S.emit("V", lambda e: e.tensor_scalar(
            u, u, -1.0, 0.25, op0=Op.mult, op1=Op.add), deps=[tsn])
        tl_ = S.emit("V", lambda e: e.tensor_scalar(
            uscr, u, 0.5, None, op0=Op.is_ge), deps=[tl_])
        tl_ = S.emit("V", lambda e: e.scalar_tensor_tensor(
            u, uscr, -1.0, u, op0=Op.mult, op1=Op.add), deps=[tl_])
        tok_tab = S.emit("A", lambda e: e.activation(
            cosb, u, AF.Sin, scale=TWO_PI), deps=[tl_])
        if tile == 0:
            dump("cos", cosb, tok_tab, [128, TH])
            dump("sin", sinb, tok_tab, [128, TH])

        tok_h_v, tok_h_a = layernorm_to(
            True, [tok_xload] + barrier, lambda ch: h_hi(ch),
            lambda ch: h_lo(ch))
        if tile == 0:
            dump("xt0", xt_ch(0), tok_xload, [128, TH])
            dump("murow", pf(O_LR, 2080), tok_h_v, [128, 2080])
            dump("istd1", pf(O_LR + 1560, 518), tok_h_v, [128, 518])
            dump("hh0", h_hi(0), tok_h_v, [128, TH], BF16)
            dump("hl0", h_lo(0), tok_h_v, [128, TH], BF16)

        for wi in range(2):
            dst = PB[7][0:16, 0:512] if wi == 0 else PB[6][0:16, 0:512]
            for ch in range(KC):
                whi = wab_h[:, wi * KC * H + ch * H: wi * KC * H + (ch + 1) * H]
                wlo = wab_l[:, wi * KC * H + ch * H: wi * KC * H + (ch + 1) * H]
                st, sp = ch == 0, ch == KC - 1
                deps = [tok_wab, tok_h_v, tok_h_a] if ch == 0 else []
                S.emit("T", lambda e, dst=dst, whi=whi, ch=ch, st=st: MM(
                    e, dst, whi, h_hi(ch, HLO, 512), st, False), deps=deps)
                S.emit("T", lambda e, dst=dst, whi=whi, ch=ch: MM(
                    e, dst, whi, h_lo(ch, HLO, 512), False, False))
                S.emit("T", lambda e, dst=dst, wlo=wlo, ch=ch, sp=sp: MM(
                    e, dst, wlo, h_hi(ch, HLO, 512), False, sp))
                if wi == 0:
                    tdst = PB[5][0:16, 0:3]
                    S.emit("T", lambda e, tdst=tdst, whi=whi, ch=ch, st=st: MM(
                        e, tdst, whi, h_hi(ch, 3, 3), st, False))
                    S.emit("T", lambda e, tdst=tdst, whi=whi, ch=ch: MM(
                        e, tdst, whi, h_lo(ch, 3, 3), False, False))
                    S.emit("T", lambda e, tdst=tdst, wlo=wlo, ch=ch, sp=sp: MM(
                        e, tdst, wlo, h_hi(ch, 3, 3), False, sp))
        tabm = S.tok("t")
        abuf = pf(O_AB, 520)
        bbuf = pf(O_AB + 520, 520)
        S.emit("A", lambda e: e.activation(abuf[0:H, HLO:TH],
                                           PB[7][0:16, 0:512], AF.Copy),
               deps=[tabm, tok_tab])
        S.emit("A", lambda e: e.activation(abuf[0:H, 3:6], PB[5][0:16, 0:3],
                                           AF.Copy), deps=[tabm])
        tok_ab = S.emit("A", lambda e: e.activation(
            bbuf[0:H, HLO:TH], PB[6][0:16, 0:512], AF.Copy), deps=[tabm])

        g64 = pf(O_G64, 2048)
        tg = []
        for s in range(4):
            tg.append(S.emit("V", lambda e, s=s: e.tensor_tensor(
                g64[0:H, 512 * s:512 * s + 512], abuf[0:H, HLO - s:TH - s],
                bbuf[0:H, HLO:TH], op=Op.mult), deps=[tok_ab]))
        tok_g64 = S.emit("A", lambda e: e.activation(
            g64[0:H, :], g64[0:H, :], AF.Sigmoid), deps=tg)

        qb = pf(O_QKV, 520)
        kb = pf(O_QKV + 520, 520)
        vb = pf(O_QKV + 1040, 520)
        qc = pf(O_QKV + 1560, 520)
        kc_ = pf(O_QKV + 2080, 520)
        prodb = pf(O_QKV + 2600, 520)
        vpost = pf(O_QKV + 3120, 520)
        D4e = pf(O_ROW, 512)
        RQ4 = pf(O_ROW + 512, 512)
        RK4 = pf(O_ROW + 1024, 512)
        G4 = pf(O_ROW + 1536, 512)
        KT = pf(O_ROW + 2048, 8)
        crow2 = pf(O_CROW, 512)
        hs = {}
        wstate["split"] = S.tok("v")
        wstate["mmfree"] = [None, None]

        SPECS = [("q", None, 0, 0), ("k", None, 1, 1), ("v", None, 0, 0)]

        def front_mm(h, idx):
            nm, dram, par, bank = [("q", d_wq, 0, 0), ("k", d_wk, 1, 1),
                                   ("v", d_wv, 0, 0)][idx]
            if idx == 2:
                wstate["mmfree"][0] = hs[("mm", h, 0)]
            whi, wlo, wdeps = load_w_slab(
                dram, h, par, extra=[tok_h_a] if h == 0 and idx == 0 else [])
            psm = PB[bank]
            prev_evac = (hs.get(("ev", h, 0)) if idx == 2 else
                         hs.get(("ev", h - 1, 2)) if idx == 0 else
                         hs.get(("ev", h - 1, 1)))
            tl2 = None
            for ch in range(KC):
                whic = whi[:, ch * 128:(ch + 1) * 128]
                wloc = wlo[:, ch * 128:(ch + 1) * 128]
                tl2 = mm3(psm[:, 0:512], whic, wloc, ch, 0, 512,
                          ch == 0, ch == KC - 1,
                          deps=(wdeps + [tok_h_v, tok_h_a, prev_evac])
                          if ch == 0 else [])
            hs[("tmain", h, idx)] = tl2
            for ch in range(KC):
                whic = whi[:, ch * 128:(ch + 1) * 128]
                wloc = wlo[:, ch * 128:(ch + 1) * 128]
                tl2 = mm3(PB[5][:, idx * 8: idx * 8 + 6], whic, wloc,
                          ch, 512, 6, ch == 0, ch == KC - 1,
                          deps=[tok_ab] if h == 0 and idx == 0 and
                          ch == 0 else [])
            hs[("mm", h, idx)] = tl2
            if idx == 1:
                wstate["mmfree"][1] = tl2
            if idx == 2:
                wstate["mmfree"][0] = tl2

        def front_evac(h, idx):
            nm = ("q", "k", "v")[idx]
            buf = {"q": qb, "k": kb, "v": vb}[nm]
            psm = PB[(0, 1, 0)[idx]]
            freed = (hs.get(("useqk", h - 1)) if idx in (0, 1) else
                     hs.get(("usev", h - 1)))
            S.emit("A", lambda e, buf=buf, psm=psm: e.activation(
                buf[:, 0:512], psm[:, 0:512], AF.Silu),
                deps=[hs[("tmain", h, idx)], freed])
            hs[("ev", h, idx)] = S.emit(
                "A", lambda e, buf=buf, idx=idx: e.activation(
                    buf[:, 512:518], PB[5][:, idx * 8: idx * 8 + 6],
                    AF.Silu), deps=[hs[("mm", h, idx)], freed])
            if idx == 2:
                hs[("front", h)] = S.tok("a")

        def conv_rope(h, pre, post, rotbuf, cwcol, do_rope, dep0):
            tc = S.emit("V", lambda e, pre=pre, post=post, cwcol=cwcol:
                        e.tensor_scalar(post[:, 0:TH], pre[:, 0:TH],
                                        cwt[:, cwcol + 3:cwcol + 4], None,
                                        op0=Op.mult), deps=dep0,
                        sync_self=True)
            for j in range(3):
                sft = 3 - j
                tc = S.emit("V", lambda e, pre=pre, post=post, j=j, sft=sft,
                            cwcol=cwcol: e.scalar_tensor_tensor(
                    post[:, sft:TH], pre[:, 0:TH - sft],
                    cwt[:, cwcol + j:cwcol + j + 1], post[:, sft:TH],
                    op0=Op.mult, op1=Op.add), deps=[tc], sync_self=True)
            if not do_rope:
                return tc
            tc = S.emit("V", lambda e, post=post, rotbuf=rotbuf:
                        e.tensor_scalar(rotbuf[0:64, 3:TH],
                                        post[64:128, 3:TH], -1.0, None,
                                        op0=Op.mult), deps=[tc],
                        sync_self=True)
            tc = S.emit("V", lambda e, post=post, rotbuf=rotbuf:
                        e.tensor_copy(rotbuf[64:128, 3:TH], post[0:64, 3:TH]),
                        deps=[tc])
            tc = S.emit("V", lambda e, post=post: e.tensor_tensor(
                post[:, 3:TH], post[:, 3:TH], cosb[:, 3:TH], op=Op.mult),
                deps=[tc, tok_tab])
            tc = S.emit("V", lambda e, rotbuf=rotbuf: e.tensor_tensor(
                rotbuf[:, 3:TH], rotbuf[:, 3:TH], sinb[:, 3:TH], op=Op.mult),
                deps=[tc])
            tc = S.emit("V", lambda e, post=post, rotbuf=rotbuf:
                        e.tensor_tensor(post[:, 3:TH], post[:, 3:TH],
                                        rotbuf[:, 3:TH], op=Op.add), deps=[tc])
            return tc

        def head_back(h):
            dep0 = [hs[("front", h)]]
            cwb = h * 12
            if tile == 0 and h == 0:
                dump("qpre0", qb, hs[("front", 0)], [128, 520])
                dump("kpre0", kb, hs[("front", 0)], [128, 520])
                dump("vpre0", vb, hs[("front", 0)], [128, 520])
            tq = conv_rope(h, qb, qc, prodb, cwb, True, dep0)
            tsq = S.emit("V", lambda e: e.tensor_tensor(
                pf(O_SQ, TH), qc[:, 0:TH], qc[:, 0:TH], op=Op.mult),
                deps=[tq], sync_self=True)
            tl2q = S.emit("T", lambda e: MM(
                e, PB[6][0:1, 0:512], ones_c[:], pf(O_SQ + 6, 512), True,
                True), deps=[tsq, hs.get(("l2e", h - 1)),
                             hs.get(("gex", h - 1)),
                             tok_ab if h == 0 else None])
            tk = conv_rope(h, kb, kc_, qb, cwb + 4, True, [tq])
            tsk = S.emit("V", lambda e: e.tensor_tensor(
                pf(O_SQ, TH), kc_[:, 0:TH], kc_[:, 0:TH], op=Op.mult),
                deps=[tk, S.tok("t")], sync_self=True)
            S.emit("T", lambda e: MM(
                e, PB[7][0:1, 0:512], ones_c[:], pf(O_SQ + 6, 512), True,
                True), deps=[tsk, hs.get(("l2e", h - 1)),
                             hs.get(("gex", h - 1)),
                             tok_ab if h == 0 else None])
            tl2k = S.emit("T", lambda e: MM(
                e, PB[5][0:1, 24:27], ones_c[:], pf(O_SQ + 3, 3), True,
                True), deps=[tsk, hs.get(("l2e", h - 1))])
            tv_ = conv_rope(h, vb, vpost, None, cwb + 8, False, [tk])
            hs[("useqk", h)] = tk
            hs[("usev", h)] = tv_
            rsqdep = [hs.get(("rsq", h - 1)), hs.get(("coef", h - 1))]
            te1 = S.emit("A", lambda e: e.activation(
                RQ4[0:1, 0:512], PB[6][0:1, 0:512], AF.Copy),
                deps=[tl2q] + rsqdep)
            te2 = S.emit("A", lambda e: e.activation(
                RK4[0:1, 0:512], PB[7][0:1, 0:512], AF.Copy),
                deps=[tl2k] + rsqdep)
            te4 = S.emit("A", lambda e: e.activation(
                KT[0:1, 0:3], PB[5][0:1, 24:27], AF.Copy), deps=[tl2k])
            hs[("l2e", h)] = S.tok("a")
            tq4 = []
            for s in range(1, 4):
                tq4.append(S.emit("V", lambda e, s=s: e.tensor_copy(
                    RQ4[32 * s:32 * s + 1, 0:512], RQ4[0:1, 0:512]),
                    deps=[te1] + rsqdep))
                tq4.append(S.emit("V", lambda e, s=s: e.tensor_copy(
                    RK4[32 * s:32 * s + 1, s:512], RK4[0:1, 0:512 - s]),
                    deps=[te2] + rsqdep))
                tq4.append(S.emit("V", lambda e, s=s: e.tensor_copy(
                    RK4[32 * s:32 * s + 1, 0:s], KT[0:1, 3 - s:3]),
                    deps=[te4] + rsqdep))
            tmx = S.emit("V", lambda e: e.tensor_scalar(
                pf(O_ROW + 512, 1024), pf(O_ROW + 512, 1024),
                1.0e-24, None, op0=Op.max), deps=tq4 + [te1, te2],
                sync_self=True)
            tr2 = tmx
            for blk_i, (ublk, fold) in enumerate(
                    ((RQ4, 1.0 / math.sqrt(128.0)), (RK4, 1.0))):
                ta1 = S.emit("A", lambda e, ublk=ublk: e.activation(
                    crow2[:, 0:512], ublk, AF.Ln),
                    deps=[tr2, hs.get(("lastst", h - 1)),
                          hs.get(("bc", h - 1, 3))])
                ta1 = S.emit("A", lambda e: e.activation(
                    D4e, crow2[:, 0:512], AF.Exp, scale=-0.5),
                    deps=[ta1, hs.get(("lastst", h - 1)),
                          hs.get(("d4f", h - 1))])
                tn = S.emit("V", lambda e: e.tensor_tensor(
                    crow2[:, 0:512], D4e, D4e, op=Op.mult), deps=[ta1])
                tn = S.emit("V", lambda e, ublk=ublk: e.tensor_tensor(
                    crow2[:, 0:512], crow2[:, 0:512], ublk, op=Op.mult),
                    deps=[tn])
                tn = S.emit("V", lambda e: e.tensor_scalar(
                    crow2[:, 0:512], crow2[:, 0:512], -0.5, 1.5,
                    op0=Op.mult, op1=Op.add), deps=[tn])
                tr2 = S.emit("V", lambda e, ublk=ublk, fold=fold:
                             e.scalar_tensor_tensor(
                                 ublk, D4e, fold, crow2[:, 0:512],
                                 op0=Op.mult, op1=Op.mult), deps=[tn])
            hs[("rsq", h)] = tr2
            tgk = []
            gexd = [hs.get(("l2e", h)), hs.get(("coef", h - 1))]
            for s in range(4):
                gps = PB[6 + (s % 2)]
                tgm = S.emit("T", lambda e, gps=gps, s=s, h=h: MM(
                    e, gps[0:1, 0:512], eye16[0:16, h:h + 1],
                    g64[0:16, 512 * s:512 * s + 512], True, True),
                    deps=[tok_g64, te1, te2] + gexd +
                    ([tgk[s - 2]] if s >= 2 else []))
                tgk.append(S.emit("A", lambda e, gps=gps, s=s: e.activation(
                    G4[32 * s:32 * s + 1, 0:512], gps[0:1, 0:512], AF.Copy),
                    deps=[tgm]))
            hs[("gex", h)] = S.tok("a")
            tdm = None
            d4dep = [hs.get(("d4e", h - 1))]
            for s in range(4):
                tp = S.emit("V", lambda e, s=s: e.tensor_tensor(
                    prodb[:, 0:512], qc[:, HLO:TH], kc_[:, HLO - s:TH - s],
                    op=Op.mult), deps=[tsk, tq, tdm] +
                    ([hs.get(("vacc", h - 1, 3))] if s == 0 else []))
                tdm = S.emit("T", lambda e, s=s: MM(
                    e, PB[4][:, 0:512], ohB[:, 128 * s:128 * s + 128],
                    prodb[:, 0:512], s == 0, s == 3),
                    deps=[tp, tok_g0] + d4dep)
            tce = S.emit("A", lambda e: e.activation(
                D4e, PB[4][:, 0:512], AF.Copy),
                deps=[tdm, hs.get(("coef", h - 1)),
                      hs.get(("lastst", h - 1)), hs[("rsq", h)]])
            hs[("d4e", h)] = tce
            hs[("d4f", h)] = tce
            tcf = S.emit("V", lambda e: e.tensor_tensor(
                D4e, D4e, RQ4, op=Op.mult), deps=[tce, tr2])
            tcf = S.emit("V", lambda e: e.tensor_tensor(
                D4e, D4e, RK4, op=Op.mult), deps=[tcf])
            tcf = S.emit("V", lambda e: e.tensor_tensor(
                D4e, D4e, G4, op=Op.mult), deps=[tcf] + tgk)
            hs[("coef", h)] = tcf
            outm = pf(O_OUT + h * 512, 512)
            for s in range(4):
                cb = PB[2 + (s % 2)]
                tst = S.emit("V", lambda e, s=s: e.tensor_copy(
                    crow2[0:1, 0:512], D4e[32 * s:32 * s + 1, 0:512]),
                    deps=[tcf] + ([hs.get(("bc", h, s - 1))] if s > 0 else
                                  [hs.get(("bc", h - 1, 3))]))
                tbc = S.emit("T", lambda e, cb=cb: MM(
                    e, cb[:, 0:512], ones_r[:], crow2[0:1, 0:512], True,
                    True), deps=[tst, hs.get(("vacc", h, s - 2)) if s >= 2
                                 else hs.get(("vacc", h - 1, s + 2))])
                hs[("bc", h, s)] = tbc
                if s == 0:
                    tva = S.emit("V", lambda e, cb=cb, outm=outm:
                                 e.tensor_tensor(outm, cb[:, 0:512],
                                                 vpost[:, HLO:TH],
                                                 op=Op.mult),
                                 deps=[tbc, tv_], sync_self=True)
                else:
                    tva = S.emit("V", lambda e, cb=cb, s=s: e.tensor_tensor(
                        prodb[:, 0:512], cb[:, 0:512],
                        vpost[:, HLO - s:TH - s],
                        op=Op.mult), deps=[tbc, tv_])
                    tva = S.emit("V", lambda e, outm=outm: e.tensor_tensor(
                        outm, outm, prodb[:, 0:512], op=Op.add), deps=[tva])
                hs[("vacc", h, s)] = tva
                hs[("lastst", h)] = tst
            if tile == 0 and h == 0:
                dump("qc0", qc, hs[("coef", 0)], [128, 520])
                dump("kc0", kc_, hs[("coef", 0)], [128, 520])
                dump("rqrk0", pf(O_ROW + 512, 1024), hs[("rsq", 0)],
                     [128, 1024])
                dump("g40", G4, hs[("gex", 0)], [128, 512])
                dump("d4e0", D4e, hs[("coef", 0)], [128, 512])
                dump("out0", pf(O_OUT, 512), hs[("vacc", 0, 3)], [128, 512])

        gb = (N_GAM // H + 1) if tile == 0 else 0
        for h in range(H):
            front_mm(h, 0)
            front_mm(h, 1)
            gamma_step(gb)
            if h > 0:
                head_back(h - 1)
            front_evac(h, 0)
            front_evac(h, 1)
            front_mm(h, 2)
            front_evac(h, 2)
        head_back(H - 1)
        tok_mix_v = S.tok("v")
        tok_mix_a = S.tok("a")
        tok_mix_t = S.tok("t")

        if tile == 0:
            gamma_step(N_GAM)
            tga = S.tok("a")
            rs = []
            for wi, (a0, b0) in enumerate(((0, 88), (88, 176), (176, N_GAM))):
                rs.append(S.emit("V", lambda e, wi=wi, a0=a0, b0=b0:
                                 e.reduce_sum(pf(GACC + 280, 8)[:, wi:wi + 1],
                                              pf(GACC + a0, b0 - a0),
                                              axis=AX.X), deps=[tga]))
            tsum = S.emit("T", lambda e: MM(
                e, PB[7][0:1, 0:3], ones_c[:],
                pf(GACC + 280, 8)[:, 0:3], True, True),
                deps=rs + [tok_mix_a, tok_mix_v])
            tth = S.emit("A", lambda e: e.activation(
                grow[0:1, 0:3], PB[7][0:1, 0:3], AF.Copy, scale=0.5 / NW,
                bias=0.5e-8), deps=[tsum])
            tgc = S.emit("T", lambda e: MM(
                e, PB[2][:, 0:3], ones_r[:], grow[0:1, 0:3], True, True),
                deps=[tth, tok_mix_v])
            tgc = S.emit("A", lambda e: e.activation(
                gam_c[:, 0:3], PB[2][:, 0:3], AF.Copy), deps=[tgc])
            tok_gam = S.emit("V", lambda e: e.tensor_scalar(
                negc[:, 0:3], gam_c[:, 0:3], -1.0, None, op0=Op.mult),
                deps=[tgc])
            build_nc.tok_gam = tok_gam
        else:
            tok_gam = build_nc.tok_gam

        sqb = [pf(O_SQ, 520), pf(O_AB + 520, 520)]
        lastq = [None, None]
        for ch in range(KC):
            sq = sqb[ch % 2]
            tsq = S.emit("V", lambda e, ch=ch, sq=sq: e.tensor_tensor(
                sq[:, 0:T], pf(O_OUT + ch * 512, 512),
                pf(O_OUT + ch * 512, 512), op=Op.mult),
                deps=([tok_mix_v] if ch == 0 else []) + [lastq[ch % 2]])
            lastq[ch % 2] = S.emit("T", lambda e, ch=ch, sq=sq: MM(
                e, PB[4][0:1, 0:512], ones_c[:], sq[:, 0:T], ch == 0,
                ch == KC - 1), deps=[tsq, tok_mix_a] if ch == 0 else [tsq])
        rmu = pf(O_LR, 520)[0:1, 0:T]
        rmsc = pf(O_LR + 520, 520)[0:1, 0:T]
        trow = S.emit("V", lambda e: e.tensor_scalar(
            rmu, PB[4][0:1, 0:512], 1.0 / DM, 1.0e-6, op0=Op.mult,
            op1=Op.add), deps=[S.tok("t")])
        tq0 = S.emit("A", lambda e: e.activation(rmsc, rmu, AF.Ln),
                     deps=[trow])
        tq0 = S.emit("A", lambda e: e.activation(rmsc, rmsc, AF.Exp,
                                                 scale=-0.5), deps=[tq0])
        tn = S.emit("V", lambda e: e.tensor_tensor(
            pf(O_LR + 1040, 520)[0:1, 0:T], rmsc, rmsc, op=Op.mult),
            deps=[tq0])
        tn = S.emit("V", lambda e: e.tensor_tensor(
            pf(O_LR + 1040, 520)[0:1, 0:T], pf(O_LR + 1040, 520)[0:1, 0:T],
            rmu, op=Op.mult), deps=[tn])
        tn = S.emit("V", lambda e: e.tensor_scalar(
            pf(O_LR + 1040, 520)[0:1, 0:T], pf(O_LR + 1040, 520)[0:1, 0:T],
            -0.5, 1.5, op0=Op.mult, op1=Op.add), deps=[tn])
        trow = S.emit("V", lambda e: e.tensor_tensor(
            rmu, rmsc, pf(O_LR + 1040, 520)[0:1, 0:T], op=Op.mult),
            deps=[tn])
        trb = S.emit("T", lambda e: MM(
            e, PB[2][:, 0:512], ones_r[:], pf(O_LR, 520)[0:1, 0:T], True,
            True), deps=[trow, tok_mix_v])
        if tile == 0:
            dump("outF", pf(O_OUT, 8192), tok_mix_v, [128, 8192])
            dump("rms", pf(O_LR, 520), trow, [128, 520])
            dump("gamc", gam_c[:], tok_gam, [128, 3])
        zsil = pf(O_ZS, 512)
        gtmp = pf(O_ZS + 512, 512)
        wstate["split"] = tok_mix_t
        wstate["mmfree"] = [tok_mix_t, tok_mix_t]
        tok_g_done = []
        for m in range(KC):
            whi, wlo, wdeps = load_w_slab(d_wz, m, m % 2)
            zp = PB[m % 2]
            tl2 = None
            for ch in range(KC):
                whic = whi[:, ch * 128:(ch + 1) * 128]
                wloc = wlo[:, ch * 128:(ch + 1) * 128]
                tl2 = mm3(zp[:, 0:512], whic, wloc, ch, HLO, 512, ch == 0,
                          ch == KC - 1,
                          deps=(wdeps + [tok_mix_v,
                                         tok_g_done[m - 2] if m >= 2 else
                                         None]) if ch == 0 else [])
            wstate["mmfree"][m % 2] = tl2
            tz = S.emit("A", lambda e, zp=zp: e.activation(
                zsil, zp[:, 0:512], AF.Silu),
                deps=[tl2, tok_g_done[m - 1] if m >= 1 else None])
            tg1 = S.emit("V", lambda e, m=m: e.tensor_tensor(
                gtmp, pf(O_OUT + m * 512, 512), PB[2][:, 0:512], op=Op.mult),
                deps=[trb, tz])
            tg1 = S.emit("V", lambda e: e.tensor_tensor(
                gtmp, gtmp, zsil, op=Op.mult), deps=[tg1])
            tg1 = S.emit("V", lambda e, m=m: e.tensor_scalar(
                gtmp, gtmp, nwt[:, m:m + 1], None, op0=Op.mult),
                deps=[tg1], sync_self=True)
            ghi = pbf(O_OUT + m * 512, 512)
            glo = pool_bf[:, 2 * (O_OUT + m * 512) + 512:
                          2 * (O_OUT + m * 512) + 1024]
            th2 = S.emit("A", lambda e, ghi=ghi: e.activation(
                ghi, gtmp, AF.Copy), deps=[tg1])
            tok_g_done.append(S.emit("V", lambda e, glo=glo, ghi=ghi:
                                     e.tensor_tensor(glo, gtmp, ghi,
                                                     op=Op.subtract),
                                     deps=[th2]))

        def g_hi(ch):
            return pbf(O_OUT + ch * 512, 512)

        def g_lo(ch):
            return pool_bf[:, 2 * (O_OUT + ch * 512) + 512:
                           2 * (O_OUT + ch * 512) + 1024]

        tok_ao = []
        for m in range(KC):
            whi, wlo, wdeps = load_w_slab(d_wo, m, m % 2)
            ap_ = PB[m % 2]
            tl2 = None
            for ch in range(KC):
                whic = whi[:, ch * 128:(ch + 1) * 128]
                wloc = wlo[:, ch * 128:(ch + 1) * 128]
                deps0 = (wdeps + [tok_g_done[-1],
                                  tok_ao[m - 2] if m >= 2 else None]) \
                    if ch == 0 else []
                S.emit("T", lambda e, ap_=ap_, whic=whic, ch=ch: MM(
                    e, ap_[:, 0:512], whic, g_hi(ch), ch == 0, False),
                    deps=deps0)
                S.emit("T", lambda e, ap_=ap_, whic=whic, ch=ch: MM(
                    e, ap_[:, 0:512], whic, g_lo(ch), False, False))
                tl2 = S.emit("T", lambda e, ap_=ap_, wloc=wloc, ch=ch: MM(
                    e, ap_[:, 0:512], wloc, g_hi(ch), False, ch == KC - 1))
            wstate["mmfree"][m % 2] = tl2
            tok_ao.append(S.emit("A", lambda e, m=m, ap_=ap_: e.activation(
                ao_ch(m), ap_[:, 0:512], AF.Copy), deps=[tl2]))

        if tile == 0:
            dump("ao0", ao_ch(0), tok_ao[0], [128, T])
        tok_res1 = resid_ln(False, [tok_ao[-1]])
        if tile == 0:
            dump("xr1", xt_ch(0), tok_res1, [128, TH])

        tok_h2_v, tok_h2_a = layernorm_to(
            False, [tok_res1], lambda ch: h_hi(ch, HLO, 512),
            lambda ch: h_lo(ch, HLO, 512))

        sigb = pf(F_SC, 512)
        sq2 = pf(F_SC + 512, 512)
        tmpb = pbf(F_SC, 2048)
        fst = {"d1": None, "d2": None, "q1mm": [None, None],
               "q2mm": [None, None], "tmpfree": None, "hidfree": None}

        def hid_hi(k):
            return pool_bf[:, 2 * (F_HID + k * 512): 2 * (F_HID + k * 512) + 512]

        def hid_lo(k):
            return pool_bf[:, 2 * (F_HID + k * 512) + 512:
                           2 * (F_HID + k * 512) + 1024]

        tok_hid = []
        tok_sg = []
        for m in range(MC):
            par = m % 2
            stg1 = pf(F_ST, 2048)
            stg2 = pf(F_ST + 2048, 2048)
            q1 = pbf(F_Q + par * 2048, 2048)
            q2 = pbf(F_Q + par * 2048 + 1024, 2048)
            d1 = S.emit("S", lambda e, m=m, stg1=stg1: e.dma_start(
                out=stg1, in_=d_w1[m * 128:(m + 1) * 128, :]),
                deps=[fst["d1"]] + ([tok_ao[-1], tok_h2_v, tok_h2_a]
                                    if m == 0 else []), dma="sd")
            d2 = S.emit("S", lambda e, m=m, stg2=stg2: e.dma_start(
                out=stg2, in_=d_w2[m * 128:(m + 1) * 128, :]),
                deps=[fst["d2"]] + ([tok_h2_v] if m == 0 else []), dma="sd")
            ts1 = S.emit("A", lambda e, q1=q1, stg1=stg1: e.activation(
                q1, stg1, AF.Sign, bias=negc[:, 0:1]),
                deps=[d1, tok_gam, fst["q1mm"][par], tok_h2_v])
            ts2 = S.emit("A", lambda e, stg1=stg1: e.activation(
                tmpb, stg1, AF.Sign, bias=gam_c[:, 0:1]),
                deps=[ts1, fst["tmpfree"]])
            tq1 = S.emit("V", lambda e, q1=q1: e.tensor_tensor(
                q1, q1, tmpb, op=Op.add), deps=[ts2])
            ts1b = S.emit("A", lambda e, q2=q2, stg2=stg2: e.activation(
                q2, stg2, AF.Sign, bias=negc[:, 1:2]),
                deps=[d2, tok_gam, fst["q2mm"][par]])
            ts2b = S.emit("A", lambda e, stg2=stg2: e.activation(
                tmpb, stg2, AF.Sign, bias=gam_c[:, 1:2]),
                deps=[ts1b, tq1])
            tq2 = S.emit("V", lambda e, q2=q2: e.tensor_tensor(
                q2, q2, tmpb, op=Op.add), deps=[ts2b])
            fst["d1"] = tq1
            fst["d2"] = tq2
            x1p = PB[par]
            x2p = PB[2 + par]
            tlast1 = None
            tlast2 = None
            for ch in range(KC):
                q1c = q1[:, ch * 128:(ch + 1) * 128]
                deps0 = [tq1, tok_h2_v, tok_h2_a,
                         tok_sg[m - 2] if m >= 2 else None] if ch == 0 else []
                S.emit("T", lambda e, x1p=x1p, q1c=q1c, ch=ch: MM(
                    e, x1p[:, 0:512], q1c, h_hi(ch, HLO, 512), ch == 0,
                    False), deps=deps0)
                tlast1 = S.emit("T", lambda e, x1p=x1p, q1c=q1c, ch=ch: MM(
                    e, x1p[:, 0:512], q1c, h_lo(ch, HLO, 512), False,
                    ch == KC - 1))
            for ch in range(KC):
                q2c = q2[:, ch * 128:(ch + 1) * 128]
                deps0 = [tq2, tok_hid[m - 2] if m >= 2 else None] \
                    if ch == 0 else []
                S.emit("T", lambda e, x2p=x2p, q2c=q2c, ch=ch: MM(
                    e, x2p[:, 0:512], q2c, h_hi(ch, HLO, 512), ch == 0,
                    False), deps=deps0)
                tlast2 = S.emit("T", lambda e, x2p=x2p, q2c=q2c, ch=ch: MM(
                    e, x2p[:, 0:512], q2c, h_lo(ch, HLO, 512), False,
                    ch == KC - 1))
            fst["q1mm"][par] = tlast1
            fst["q2mm"][par] = tlast2
            tsg = S.emit("A", lambda e, x1p=x1p: e.activation(
                sigb, x1p[:, 0:512], AF.Sigmoid, scale=0.5),
                deps=[tlast1, tq2, fst["hidfree"]])
            tok_sg.append(tsg)
            thd = S.emit("V", lambda e, x2p=x2p: e.scalar_tensor_tensor(
                sq2, x2p[:, 0:512], 0.5, sigb, op0=Op.mult, op1=Op.mult),
                deps=[tsg, tlast2])
            thh = S.emit("A", lambda e, m=m: e.activation(
                hid_hi(m), sq2, AF.Copy), deps=[thd])
            tlo = S.emit("V", lambda e, m=m: e.tensor_tensor(
                hid_lo(m), sq2, hid_hi(m), op=Op.subtract), deps=[thh])
            tok_hid.append(tlo)
            fst["tmpfree"] = tsg
            fst["hidfree"] = tlo
        tok_hid_all = S.tok("v")

        w3s = {"d": [None, None], "q": [None, None]}
        tok_ao2 = []
        for m in range(KC):
            ap2 = PB[4 + (m % 2)] if False else PB[m % 2]
            tl2 = None
            for half in range(4):
                p2 = half % 2
                k0 = half * 11
                stg = pf(W3_ST + p2 * 1408, 1408)
                q3 = pbf(W3_Q + p2 * 704, 1408)
                d3 = S.emit("S", lambda e, m=m, k0=k0, stg=stg: e.dma_start(
                    out=stg, in_=d_w3[m * 128:(m + 1) * 128,
                                      k0 * 128:(k0 + 11) * 128]),
                    deps=[w3s["d"][p2]] + ([tok_hid_all, tok_sg[-1]]
                                           if m == 0 and half == 0 else []),
                    dma="sd")
                tsa = S.emit("A", lambda e, q3=q3, stg=stg: e.activation(
                    q3, stg, AF.Sign, bias=negc[:, 2:3]),
                    deps=[d3, tok_gam, w3s["q"][p2]])
                tsb = S.emit("A", lambda e, stg=stg: e.activation(
                    tmpb[:, 0:1408], stg, AF.Sign, bias=gam_c[:, 2:3]),
                    deps=[tsa, tok_sg[-1] if m == 0 and half == 0 else None])
                tq3 = S.emit("V", lambda e, q3=q3: e.tensor_tensor(
                    q3, q3, tmpb[:, 0:1408], op=Op.add), deps=[tsb])
                w3s["d"][p2] = tq3
                for kk in range(11):
                    k = k0 + kk
                    q3c = q3[:, kk * 128:(kk + 1) * 128]
                    deps0 = [tq3] + ([tok_ao2[m - 2]] if m >= 2 and k == 0
                                     else [])
                    S.emit("T", lambda e, ap2=ap2, q3c=q3c, k=k: MM(
                        e, ap2[:, 0:512], q3c, hid_hi(k), k == 0, False),
                        deps=deps0)
                    tl2 = S.emit("T", lambda e, ap2=ap2, q3c=q3c, k=k: MM(
                        e, ap2[:, 0:512], q3c, hid_lo(k), False,
                        k == MC - 1))
                w3s["q"][p2] = tl2
            tok_ao2.append(S.emit("A", lambda e, m=m, ap2=ap2: e.activation(
                ao_ch(m), ap2[:, 0:512], AF.Copy), deps=[tl2]))

        if tile == 0:
            dump("hid0", pool_bf[:, 0:1024], tok_hid[0], [128, 1024], BF16)
            dump("ao20", ao_ch(0), tok_ao2[0], [128, T])
        tok_res2 = resid_ln(True, [tok_ao2[-1]])
        if tile == 0:
            dump("xf0", xt_ch(0), tok_res2, [128, TH])

        for ch in range(KC):
            src = xt_ch(ch, HLO, T)
            dst = d_out[ch * 128:(ch + 1) * 128, tile * T:(tile + 1) * T]
            S.emit("G", lambda e, src=src, dst=dst: e.dma_start(
                out=dst, in_=src), deps=[tok_res2], dma="gd")
        barrier = [S.tok("t"), S.tok("v"), S.tok("a"), S.tok("sd"),
                   S.tok("gd")]
        S.flush(blk)

    S.emit("G", lambda e: e.memset(fin[:], 1.0), deps=[S.tok("gd")])
    S.flush(blk)

    es.close()
    return nc


def _stage_kxm(w, kc, mc):
    return np.ascontiguousarray(
        np.asarray(w, np.float32).reshape(kc, 128, mc, 128)
        .transpose(2, 1, 0, 3).reshape(mc * 128, kc * 128))


def kernel(L0_attn_x, L0_attn_Wq, L0_attn_Wk, L0_attn_Wv, L0_attn_Wz,
           L0_attn_Wa, L0_attn_Wb, L0_attn_conv_q_w, L0_attn_conv_k_w,
           L0_attn_conv_v_w, L0_attn_norm_w, L0_attn_Wout,
           L0_ffn_w1, L0_ffn_w2, L0_ffn_w3):
    x = np.ascontiguousarray(np.asarray(L0_attn_x, dtype=np.float32))
    B, Sq, _ = x.shape
    x2d = x.reshape(B * Sq, DM)

    wq = _stage_kxm(L0_attn_Wq, KC, KC)
    wk = _stage_kxm(L0_attn_Wk, KC, KC)
    wv = _stage_kxm(L0_attn_Wv, KC, KC)
    wz = _stage_kxm(L0_attn_Wz, KC, KC)
    wo = _stage_kxm(L0_attn_Wout, KC, KC)
    w1t = _stage_kxm(np.ascontiguousarray(np.asarray(L0_ffn_w1, np.float32).T),
                     KC, MC)
    w2t = _stage_kxm(np.ascontiguousarray(np.asarray(L0_ffn_w2, np.float32).T),
                     KC, MC)
    w3t = _stage_kxm(np.ascontiguousarray(np.asarray(L0_ffn_w3, np.float32).T),
                     MC, KC)
    wa = np.asarray(L0_attn_Wa, np.float32).reshape(KC, 128, H).transpose(
        1, 0, 2).reshape(128, KC * H)
    wb = np.asarray(L0_attn_Wb, np.float32).reshape(KC, 128, H).transpose(
        1, 0, 2).reshape(128, KC * H)
    wab = np.ascontiguousarray(np.concatenate([wa, wb], axis=1))
    cw = np.concatenate(
        [np.asarray(L0_attn_conv_q_w, np.float32).reshape(H, 128, 4),
         np.asarray(L0_attn_conv_k_w, np.float32).reshape(H, 128, 4),
         np.asarray(L0_attn_conv_v_w, np.float32).reshape(H, 128, 4)],
        axis=2)
    cw = np.ascontiguousarray(cw.transpose(1, 0, 2).reshape(128, H * 12))
    nw = np.ascontiguousarray(
        np.asarray(L0_attn_norm_w, np.float32).reshape(KC, 128).T)

    nc = build_nc()

    in_maps = []
    for c in range(8):
        b = c // 4
        s0 = (c % 4) * TOK
        xs = np.zeros((HLO + TOK, DM), np.float32)
        r0 = b * Sq + s0
        xs[HLO:] = x2d[r0:r0 + TOK]
        if s0 > 0:
            xs[:HLO] = x2d[r0 - HLO:r0]
        in_maps.append({
            "xt": np.ascontiguousarray(xs.T), "wq": wq, "wk": wk, "wv": wv,
            "wz": wz, "wo": wo, "wab": wab, "cw": cw, "nw": nw,
            "pos0": np.full((128, 1), float(s0 - HLO), np.float32),
            "w1t": w1t, "w2t": w2t, "w3t": w3t,
        })

    res = run_bass_kernel_spmd(nc, in_maps, core_ids=list(range(8)))

    out = np.empty((B, Sq, DM), np.float32)
    for c in range(8):
        b = c // 4
        s0 = (c % 4) * TOK
        out[b, s0:s0 + TOK] = res.results[c]["out"].T
    return out
